# version 17
# baseline (speedup 1.0000x reference)
import math
import numpy as np
import ml_dtypes
import concourse.bass as bass
import concourse.mybir as mybir
from concourse.bass_utils import run_bass_kernel_spmd

F32 = mybir.dt.float32
BF16 = mybir.dt.bfloat16
AF = mybir.ActivationFunctionType
ALU = mybir.AluOpType

NCORE = 8
D = 4096
TOK = 1024
CH = 512
NCH = TOK // CH
DT = D // 128
NH = 32
NG = 8
NK = 64
NKEY = 8192
SLAB = 16
NSLAB = NK // SLAB
NPS = 6
ISQ = 1.0 / math.sqrt(128.0)
SHIFT = 12.0
EPS = 1e-5


class Sem:
    def __init__(self, name):
        self.name = name
        self.val = 0
        self.h = None


class Stream:
    def __init__(self, name):
        self.name = name
        self.ops = []
        self.waited = {}

    def add(self, fn, waits=(), inc=None, inc_amt=1):
        real = []
        for w in waits:
            if w is None:
                continue
            s, v = w
            if v is None or v <= 0:
                continue
            if self.waited.get(s.name, -1) >= v:
                continue
            self.waited[s.name] = v
            real.append((s, v))
        post = None
        if inc is not None:
            inc.val += inc_amt
            post = inc.val
        self.ops.append((fn, real, inc, inc_amt))
        return post

    def replay(self, eng):
        for fn, waits, inc, amt in self.ops:
            for s, v in waits:
                eng.wait_ge(s.h, v)
            if fn is None:
                continue
            instr = fn(eng)
            if inc is not None:
                instr.then_inc(inc.h, amt)


def build_program():
    nc = bass.Bass("TRN2", target_bir_lowering=False)

    x_d = nc.dram_tensor("x", [D, TOK], F32, kind="ExternalInput").ap()
    kt_d = nc.dram_tensor("kt", [NG * 128, NKEY], F32, kind="ExternalInput").ap()
    vt_d = nc.dram_tensor("vt", [NG * NKEY, 128], F32, kind="ExternalInput").ap()
    wq_d = nc.dram_tensor("wq", [D, D], F32, kind="ExternalInput").ap()
    wo_d = nc.dram_tensor("wo", [D, D], F32, kind="ExternalInput").ap()
    at_d = nc.dram_tensor("at", [D, 128], F32, kind="ExternalInput").ap()
    bt_d = nc.dram_tensor("bt", [128, D], F32, kind="ExternalInput").ap()
    wrt_d = nc.dram_tensor("wrt", [D, 8], F32, kind="ExternalInput").ap()
    br_d = nc.dram_tensor("br", [128, 1], F32, kind="ExternalInput").ap()
    rep_d = nc.dram_tensor("rep", [128, 128], F32, kind="ExternalInput").ap()
    rpm_d = nc.dram_tensor("rpm", [128, 128], F32, kind="ExternalInput").ap()
    ropc_d = nc.dram_tensor("ropc", [128, TOK], F32, kind="ExternalInput").ap()
    rops_d = nc.dram_tensor("rops", [128, TOK], F32, kind="ExternalInput").ap()
    y_d = nc.dram_tensor("y", [D, TOK], F32, kind="ExternalOutput").ap()
    wqb_d = nc.dram_tensor("wqb", [2 * DT, 128, 2048], BF16).ap()
    wob_d = nc.dram_tensor("wob", [2 * DT, 128, 2048], BF16).ap()

    sPE, sPS, sAC, sDV, sPO = (Sem(n) for n in
        ("sPE", "sPS", "sAC", "sDV", "sPO"))
    sW0, sWB0, sWB1, sIN, sRP, sXL, sST = (Sem(n) for n in
        ("sW0", "sWB0", "sWB1", "sIN", "sRP", "sXL", "sST"))
    sKT = [Sem(f"sKT{i}") for i in range(NSLAB)]
    sVT = [Sem(f"sVT{i}") for i in range(NSLAB)]
    sPCI, sPCOq, sPCOo, sWS0, sWS1 = (Sem(n) for n in
        ("sPCI", "sPCOq", "sPCOo", "sWS0", "sWS1"))
    DMAS = [sW0, sWB0, sWB1, sIN, sRP, sXL, sST,
            sPCI, sPCOq, sPCOo, sWS0, sWS1] + sKT + sVT

    PE, AC, DV, GP, SY = (Stream(n) for n in
        ("tensor", "scalar", "vector", "gpsimd", "sync"))

    from contextlib import ExitStack
    ctx = ExitStack()
    with ctx:
        def sb(name, shape, dt):
            return ctx.enter_context(nc.sbuf_tensor(name, shape, dt))

        X = sb("X", [128, DT * CH], F32)
        QC = sb("QC", [128, DT * CH], BF16)
        XN = sb("XN", [128, DT * CH], BF16)
        KT = sb("KT", [128, NKEY], BF16)
        VT = sb("VT", [128, NKEY], BF16)
        WB = sb("WB", [128, 4096], BF16)
        ATW = sb("ATW", [128, DT * 128], BF16)
        WRT = sb("WRT", [128, DT * 8], BF16)
        RPC = sb("RPC", [128, CH], F32)
        RPS = sb("RPS", [128, CH], F32)
        RPM = sb("RPM", [128, 128], BF16)
        REP = sb("REP", [128, 128], BF16)
        ONES = sb("ONES", [128, 128], BF16)
        OO8 = sb("OO8", [8, 128], BF16)
        BRc = sb("BRc", [128, 1], F32)
        CST = sb("CST", [128, 4], F32)
        MB = sb("MB", [128, CH], F32)
        RSD = sb("RSD", [128, CH], F32)
        MBB = sb("MBB", [128, CH], BF16)
        RSDB = sb("RSDB", [128, CH], BF16)
        G0 = sb("G0", [128, CH], F32)
        SQr = sb("SQr", [128, CH], BF16)
        TTr = sb("TTr", [128, 2 * CH], BF16)
        UR = sb("UR", [128, CH], F32)
        PS4 = sb("PS4", [128, NPS * CH], BF16)
        E8 = sb("E8", [8, CH], BF16)
        PCB = sb("PCB", [128, 2048], BF16)
        psA = ctx.enter_context(nc.psum_tensor("psA", [128, 1024], F32))
        psB = ctx.enter_context(nc.psum_tensor("psB", [128, 1024], F32))
        pc0 = ctx.enter_context(nc.psum_tensor("pc0", [128, 512], F32))
        pc1 = ctx.enter_context(nc.psum_tensor("pc1", [128, 512], F32))
        pl0 = ctx.enter_context(nc.psum_tensor("pl0", [128, 512], F32))
        pl1 = ctx.enter_context(nc.psum_tensor("pl1", [128, 512], F32))
        for s in [sPE, sPS, sAC, sDV, sPO] + DMAS:
            s.h = ctx.enter_context(nc.semaphore(s.name))
        block = ctx.enter_context(nc.Block())

        BNK = {0: psA[:, 0:512], 1: psA[:, 512:1024],
               6: psB[:, 0:512], 7: psB[:, 512:1024],
               2: pc0[:, :], 3: pc1[:, :], 4: pl0[:, :], 5: pl1[:, :]}

        free = {}

        def bw(b):
            return free.get(("pb", b))

        def tile(buf, i, w=CH):
            return buf[:, i * w:(i + 1) * w]

        DV.add(lambda v: v.memset(ONES[:, :], 1.0))
        DV.add(lambda v: v.memset(OO8[:, :], 0.125))
        DV.add(lambda v: v.memset(CST[:, 0:1], -SHIFT))
        DV.add(lambda v: v.memset(CST[:, 1:2], EPS))
        dv_const = DV.add(lambda v: v.memset(CST[:, 2:3], 1.0 / D), inc=sDV)

        SY.add(lambda s: s.dma_start(
            X[:, :].rearrange("p (i t) -> p i t", i=DT),
            x_d.rearrange("(i p) t -> p i t", p=128)[:, :, 0:CH]),
            inc=sIN, inc_amt=16)
        SY.add(lambda s: s.dma_start(RPC[:, :], ropc_d[:, 0:CH]), inc=sIN, inc_amt=16)
        SY.add(lambda s: s.dma_start(RPS[:, :], rops_d[:, 0:CH]), inc=sIN, inc_amt=16)
        ds_in0 = SY.add(lambda s: s.dma_start(BRc[:, :], br_d), inc=sIN, inc_amt=16)
        ds_x0 = ds_rope0 = ds_br = ds_in0

        GP.add(lambda g: g.dma_start(
            ATW[:, :].rearrange("p (j m) -> p j m", j=DT),
            at_d.rearrange("(j p) m -> p j m", p=128)), inc=sW0, inc_amt=16)
        GP.add(lambda g: g.dma_start(
            WRT[:, :].rearrange("p (j h) -> p j h", j=DT),
            wrt_d.rearrange("(j p) h -> p j h", p=128)), inc=sW0, inc_amt=16)
        GP.add(lambda g: g.dma_start(REP[:, :], rep_d), inc=sW0, inc_amt=16)
        dg_w0v = GP.add(lambda g: g.dma_start(RPM[:, :], rpm_d), inc=sW0, inc_amt=16)
        dg_w0 = None

        x_state = [[(sIN, ds_in0)] * DT, [None] * DT]
        rope_ready = [(sIN, ds_in0), None]
        dv_ctx = {}

        def emit_LN(c):
            pe_s1 = pe_s2 = None
            cast_v = {}
            s2_v = {}
            for i in range(DT):
                cast_v[i] = DV.add(
                    (lambda i=i: lambda v: v.tensor_copy(tile(XN, i), tile(X, i)))(),
                    waits=[x_state[c][i], free.get(("xn", i))], inc=sDV)
                free[("xrd", i)] = (sDV, cast_v[i])
                po_sq = GP.add(
                    (lambda i=i: lambda g: g.tensor_tensor(
                        SQr[:, :], tile(XN, i), tile(XN, i), ALU.mult))(),
                    waits=[(sDV, cast_v[i]), free.get(("sq",))], inc=sPO)
                w1 = [(sDV, cast_v[i])]
                if i == 0:
                    w1 += [bw(4), (sDV, dv_const)]
                pe_s1 = PE.add(
                    (lambda i=i: lambda t: t.matmul(
                        BNK[4], ONES[:, :], tile(XN, i),
                        start=(i == 0), stop=(i == DT - 1)))(),
                    waits=w1, inc=sPE)
                w2 = [(sPO, po_sq)]
                if i == 0:
                    w2 += [bw(5)]
                pe_s2 = PE.add(
                    (lambda i=i: lambda t: t.matmul(
                        BNK[5], ONES[:, :], SQr[:, :],
                        start=(i == 0), stop=(i == DT - 1)))(),
                    waits=w2, inc=sPE)
                s2_v[i] = pe_s2
                free[("sq",)] = (sPE, pe_s2)
            dv_m = DV.add(lambda v: v.tensor_scalar(
                MB[:, :], BNK[4], CST[:, 2:3], None, ALU.mult),
                waits=[(sPE, pe_s1), free.get(("mb",))], inc=sDV)
            dv_mb = DV.add(lambda v: v.tensor_copy(MBB[:, :], MB[:, :]),
                           waits=[(sDV, dv_m), free.get(("mbb",)),
                                  free.get(("mbb2",))], inc=sDV)
            dv_t2 = DV.add(lambda v: v.tensor_tensor(
                G0[:, :], MB[:, :], MB[:, :], ALU.mult),
                waits=[(sDV, dv_m), free.get(("g0",))], inc=sDV)
            dv_v = DV.add(lambda v: v.scalar_tensor_tensor(
                RSD[:, :], BNK[5], CST[:, 2:3], G0[:, :], ALU.mult, ALU.subtract),
                waits=[(sPE, pe_s2), (sDV, dv_t2), free.get(("rsd",))], inc=sDV)
            free[("pb", 4)] = (sDV, dv_m)
            free[("pb", 5)] = (sDV, dv_v)
            free[("mb",)] = (sDV, dv_t2)
            ac_ln = AC.add(lambda sc: sc.activation(
                G0[:, :], RSD[:, :], AF.Ln, bias=CST[:, 1:2]),
                waits=[(sDV, dv_v)], inc=sAC)
            free[("rsd",)] = (sAC, ac_ln)
            ac_r = AC.add(lambda sc: sc.activation(
                RSDB[:, :], G0[:, :], AF.Exp, scale=-0.5),
                waits=[(sAC, ac_ln), free.get(("rsdb",)),
                       free.get(("rsdb2",))], inc=sAC)
            free[("g0",)] = (sAC, ac_r)
            dv_xn = {}
            last_pool = last_dv = None
            for i in range(DT):
                onpool = (i % 2 == 0)
                E = GP if onpool else DV
                sem = sPO if onpool else sDV
                sl = 0 if onpool else 1
                wsub = [(sPE, s2_v[i]), (sDV, dv_mb), free.get(("tts", sl))]
                v_sub = E.add(
                    (lambda i=i, sl=sl: lambda e: e.tensor_tensor(
                        tile(TTr, sl), tile(XN, i), MBB[:, :], ALU.subtract))(),
                    waits=wsub, inc=sem)
                v_mul = E.add(
                    (lambda i=i, sl=sl: lambda e: e.tensor_tensor(
                        tile(XN, i), tile(TTr, sl), RSDB[:, :], ALU.mult))(),
                    waits=[(sem, v_sub), (sAC, ac_r)], inc=sem)
                free[("tts", sl)] = (sem, v_mul)
                dv_xn[i] = (sem, v_mul)
                if onpool:
                    last_pool = v_mul
                else:
                    last_dv = v_mul
            free[("mbb",)] = (sPO, last_pool)
            free[("mbb2",)] = (sDV, last_dv)
            free[("rsdb",)] = (sPO, last_pool)
            free[("rsdb2",)] = (sDV, last_dv)
            return dv_xn

        def emit_FFN(c, dv_xn):
            pe_lg = None
            for j in range(DT):
                w = [dv_xn[j]]
                if j == 0:
                    w += [(sW0, dg_w0v), bw(2)]
                pe_lg = PE.add(
                    (lambda j=j: lambda t: t.matmul(
                        BNK[2][:8, :], WRT[:, j * 8:(j + 1) * 8], tile(XN, j),
                        start=(j == 0), stop=(j == DT - 1)))(),
                    waits=w, inc=sPE)
                free[("xn", j)] = (sPE, pe_lg)
            ac_e = AC.add(lambda sc: sc.activation(
                E8[:, :], BNK[2][:8, :], AF.Exp, bias=BRc[:8, :]),
                waits=[(sPE, pe_lg), (sIN, ds_in0), free.get(("e8",))], inc=sAC)
            free[("pb", 2)] = (sAC, ac_e)
            pe_s8 = PE.add(lambda t: t.matmul(
                BNK[4], OO8[:, :], E8[:, :], start=True, stop=True),
                waits=[(sAC, ac_e), bw(4)], inc=sPE)
            pe_eb = PE.add(lambda t: t.matmul(
                BNK[5], REP[:8, :], E8[:, :], start=True, stop=True),
                waits=[bw(5)], inc=sPE)
            free[("e8",)] = (sPE, pe_eb)
            pe_low = None
            for j in range(DT):
                w = [dv_xn[j]] + ([bw(3)] if j == 0 else [])
                pe_low = PE.add(
                    (lambda j=j: lambda t: t.matmul(
                        BNK[3], ATW[:, j * 128:(j + 1) * 128], tile(XN, j),
                        start=(j == 0), stop=(j == DT - 1)))(),
                    waits=w, inc=sPE)
                free[("xn", j)] = (sPE, pe_low)
            dv_rb = DV.add(lambda v: v.reciprocal(RSD[:, :], BNK[4]),
                           waits=[(sPE, pe_s8), free.get(("rsd",))], inc=sDV)
            free[("pb", 4)] = (sDV, dv_rb)
            dv_t1 = DV.add(lambda v: v.tensor_tensor(
                G0[:, :], BNK[5], RSD[:, :], ALU.mult),
                waits=[(sPE, pe_eb), (sDV, dv_rb), free.get(("g0",))], inc=sDV)
            free[("pb", 5)] = (sDV, dv_t1)
            free[("rsd",)] = (sDV, dv_t1)
            lowp = PS4[:, 0:CH]
            dv_lp = DV.add(lambda v: v.tensor_tensor(
                lowp, BNK[3], G0[:, :], ALU.mult),
                waits=[(sPE, pe_low), (sDV, dv_t1), free.get(("ps4", 0))], inc=sDV)
            free[("pb", 3)] = (sDV, dv_lp)
            free[("g0",)] = (sDV, dv_lp)
            bthalf = []
            for hh in range(2):
                wsem = sWB0 if hh == 0 else sWB1
                dg_bt = GP.add(
                    (lambda hh=hh: lambda g: g.dma_start(
                        WB[:, hh * 2048:(hh + 1) * 2048],
                        bt_d[:, hh * 2048:(hh + 1) * 2048]))(),
                    waits=[free.get(("wb", hh))], inc=wsem, inc_amt=16)
                bthalf.append((wsem, dg_bt))
            obank = [0, 1, 6, 7]
            pe_half = [None, None]
            pe_last = None
            for i in range(DT):
                b = obank[i % 4]
                hh = 0 if i < 16 else 1
                w = [(sDV, dv_lp), bw(b)]
                if i in (0, 16):
                    w.append(bthalf[hh])
                pe_o = PE.add(
                    (lambda i=i, b=b, hh=hh: lambda t: t.matmul(
                        BNK[b],
                        WB[:, hh * 2048 + (i % 16) * 128:
                              hh * 2048 + (i % 16) * 128 + 128],
                        lowp, start=True, stop=True))(),
                    waits=w, inc=sPE)
                pe_half[hh] = pe_o
                pe_last = pe_o
                dv_add = DV.add(
                    (lambda i=i, b=b: lambda v: v.tensor_tensor(
                        tile(X, i), BNK[b], tile(X, i), ALU.add))(),
                    waits=[(sPE, pe_o), free.get(("xrd", i))], inc=sDV)
                free[("pb", b)] = (sDV, dv_add)
                x_state[c][i] = (sDV, dv_add)
            free[("ps4", 0)] = (sPE, pe_last)
            free[("wb", 0)] = (sPE, pe_half[0])
            free[("wb", 1)] = (sPE, pe_half[1])

        def emit_QT(c, dv_xn, cb_pre):
            po_qt = {}
            dv_qe = {}
            pe_qv = {}

            def rope_tail(i):
                rb_ = 6 + (i % 2)
                pe_rot = PE.add(
                    (lambda i=i, rb_=rb_: lambda t: t.matmul(
                        BNK[rb_], RPM[:, :], tile(QC, i),
                        start=True, stop=True))(),
                    waits=[(sDV, dv_qe[i]), bw(rb_), (sW0, dg_w0v)], inc=sPE)
                dv_t = DV.add(
                    (lambda i=i: lambda v: v.tensor_tensor(
                        G0[:, :], tile(QC, i), RPC[:, :], ALU.mult))(),
                    waits=[(sDV, dv_qe[i]), rope_ready[c], free.get(("g0",))],
                    inc=sDV)
                dv_u = DV.add(
                    (lambda rb_=rb_: lambda v: v.tensor_tensor(
                        UR[:, :], BNK[rb_], RPS[:, :], ALU.mult))(),
                    waits=[(sPE, pe_rot), free.get(("ur",))], inc=sDV)
                free[("pb", rb_)] = (sDV, dv_u)
                po_qt[i] = DV.add(
                    (lambda i=i: lambda v: v.tensor_tensor(
                        tile(QC, i), G0[:, :], UR[:, :], ALU.add))(),
                    waits=[(sDV, dv_t), (sDV, dv_u)], inc=sDV)
                free[("ur",)] = (sDV, po_qt[i])
                free[("g0",)] = (sDV, po_qt[i])

            for i in range(DT):
                if i == 0 and cb_pre is not None:
                    dghalf = cb_pre
                elif c > 0:
                    dghalf = []
                    for hh in range(2):
                        wsem = sWS0 if hh == 0 else sWS1
                        dg_cb = AC.add(
                            (lambda i=i, hh=hh: lambda sc: sc.dma_start(
                                WB[:, hh * 2048:(hh + 1) * 2048],
                                wqb_d[2 * i + hh]))(),
                            waits=[free.get(("wb", hh)), pco_done.get("q")],
                            inc=wsem, inc_amt=16)
                        dghalf.append((wsem, dg_cb))
                else:
                    dghalf = []
                    for hh in range(2):
                        wsem = sWB0 if hh == 0 else sWB1
                        dg_cb = GP.add(
                            (lambda i=i, hh=hh: lambda g: g.dma_start(
                                WB[:, hh * 2048:(hh + 1) * 2048]
                                .rearrange("p (j m) -> p j m", j=16),
                                wq_d.rearrange("(j p) m -> p j m", p=128)
                                [:, hh * 16:(hh + 1) * 16, i * 128:(i + 1) * 128]))(),
                            waits=[free.get(("wb", hh))], inc=wsem, inc_amt=16)
                        dghalf.append((wsem, dg_cb))
                b = i % 2
                pe_q = None
                for j in range(DT):
                    w = [dv_xn[j]]
                    if j == 0:
                        w += [dghalf[0], bw(b)]
                    elif j == 16:
                        w += [dghalf[1]]
                    pe_q = PE.add(
                        (lambda i=i, j=j, b=b: lambda t: t.matmul(
                            BNK[b],
                            WB[:, (0 if j < 16 else 2048) + (j % 16) * 128:
                                  (0 if j < 16 else 2048) + (j % 16) * 128 + 128],
                            tile(XN, j), start=(j == 0), stop=(j == DT - 1)))(),
                        waits=w, inc=(sPE if j in (15, DT - 1) else None))
                    if j == 15:
                        free[("wb", 0)] = (sPE, pe_q)
                free[("wb", 1)] = (sPE, pe_q)
                pe_qv[i] = pe_q
                dv_qe[i] = DV.add(
                    (lambda i=i, b=b: lambda v: v.tensor_copy(
                        tile(QC, i), BNK[b]))(),
                    waits=[(sPE, pe_q), free.get(("qc_all",))], inc=sDV)
                free[("pb", b)] = (sDV, dv_qe[i])
                if i >= 1:
                    rope_tail(i - 1)
            rope_tail(DT - 1)
            free[("rope",)] = (sDV, po_qt[DT - 1])
            for j in range(DT):
                free[("xn", j)] = (sPE, pe_qv[DT - 1])
            return po_qt

        pc_queue = []
        pco_done = {}

        def precast_pump(n=1):
            for _ in range(n):
                if not pc_queue:
                    return
                which, blk = pc_queue.pop(0)
                i, hh = blk // 2, blk % 2
                w_d = wo_d if which == "o" else wq_d
                dst_d = wob_d if which == "o" else wqb_d
                osem = sPCOo if which == "o" else sPCOq
                GP.add(
                    (lambda i=i, hh=hh, w_d=w_d: lambda g: g.dma_start(
                        PCB[:, :].rearrange("p (j m) -> p j m", j=16),
                        w_d.rearrange("(j p) m -> p j m", p=128)
                        [:, hh * 16:(hh + 1) * 16, i * 128:(i + 1) * 128]))(),
                    waits=[free.get(("pcb",))], inc=sPCI, inc_amt=16)
                v = SY.add(
                    (lambda blk=blk, dst_d=dst_d: lambda s: s.dma_start(
                        dst_d[blk], PCB[:, :]))(),
                    waits=[(sPCI, sPCI.val)], inc=osem, inc_amt=16)
                free[("pcb",)] = (osem, v)
                pco_done[which] = (osem, v)

        def emit_kv_slab(c, g, s, ktv, vtv):
            v = GP.add(
                (lambda g=g, s=s: lambda gp: gp.dma_start(
                    KT[:, s * SLAB * 128:(s + 1) * SLAB * 128],
                    kt_d[g * 128:(g + 1) * 128,
                         s * SLAB * 128:(s + 1) * SLAB * 128]))(),
                waits=[free.get(("kvslab", s))], inc=sKT[s], inc_amt=16)
            ktv[(g, s)] = (sKT[s], v)
            v = GP.add(
                (lambda g=g, s=s: lambda gp: gp.dma_start(
                    VT[:, s * SLAB * 128:(s + 1) * SLAB * 128]
                    .rearrange("p (j d) -> p j d", j=SLAB),
                    vt_d.rearrange("(j p) d -> p j d", p=128)
                    [:, g * NK + s * SLAB:g * NK + (s + 1) * SLAB, :]))(),
                waits=[free.get(("kvslab", s))], inc=sVT[s], inc_amt=16)
            vtv[(g, s)] = (sVT[s], v)

        def emit_kv_dma(c, g, ktv, vtv):
            for s in range(NSLAB):
                emit_kv_slab(c, g, s, ktv, vtv)

        flushctr = [0]

        def emit_ATT(c, po_qt):
            ktv, vtv = {}, {}
            emit_kv_dma(c, 0, ktv, vtv)
            ggl = [0]
            pending = []

            def emit_scores_group(g, qh, grp):
                h = g * 4 + qh
                gg = ggl[0]
                ggl[0] += 1
                lo = 0 if (gg % 2 == 0) else 6
                ps_val = None
                for t in range(2):
                    j = 2 * grp + t
                    w = [ktv[(g, j // SLAB)]]
                    if t == 0:
                        w += [bw(lo), bw(lo + 1), (sDV, po_qt[h])]
                    ps_val = PE.add(
                        (lambda j=j, b=lo + t, h=h: lambda tt: tt.matmul(
                            BNK[b], KT[:, j * 128:(j + 1) * 128],
                            tile(QC, h), start=True, stop=True))(),
                        waits=w, inc=(sPS if t == 1 else None))
                slot = (2 * gg) % NPS
                big = psA if (gg % 2 == 0) else psB
                ac_e = AC.add(
                    (lambda slot=slot, big=big: lambda sc: sc.activation(
                        PS4[:, slot * CH:(slot + 2) * CH], big[:, :], AF.Exp,
                        bias=CST[:, 0:1], scale=ISQ))(),
                    waits=[(sPS, ps_val)], inc=sAC)
                free[("pb", lo)] = (sAC, ac_e)
                free[("pb", lo + 1)] = (sAC, ac_e)
                return gg, ac_e

            def flush_group(item):
                g, qh, grp, gg, ac_e = item
                h = g * 4 + qh
                cb = 2 + (h % 2)
                lb = 4 + (h % 2)
                slot = (2 * gg) % NPS
                pe_L = pe_ctx = None
                for t in range(2):
                    j = 2 * grp + t
                    wl = [(sAC, ac_e)]
                    if j == 0:
                        wl.append(bw(lb))
                    pe_L = PE.add(
                        (lambda lb=lb, slot=slot, t=t, j=j: lambda tt: tt.matmul(
                            BNK[lb], ONES[:, :],
                            PS4[:, (slot + t) * CH:(slot + t + 1) * CH],
                            start=(j == 0), stop=(j == NK - 1)))(),
                        waits=wl, inc=(sPE if j == NK - 1 else None))
                    wc = [vtv[(g, j // SLAB)]]
                    if j == 0:
                        wc.append(bw(cb))
                    need_inc = (t == 1) and (
                        (qh == 3 and (j % SLAB) == SLAB - 1)
                        or grp == NK // 2 - 1)
                    pe_ctx = PE.add(
                        (lambda g=g, j=j, slot=slot, t=t, cb=cb: lambda tt: tt.matmul(
                            BNK[cb], VT[:, j * 128:(j + 1) * 128],
                            PS4[:, (slot + t) * CH:(slot + t + 1) * CH],
                            start=(j == 0), stop=(j == NK - 1)))(),
                        waits=wc, inc=(sPE if need_inc else None))
                    if qh == 3 and t == 1 and (j % SLAB) == SLAB - 1:
                        free[("kvslab", j // SLAB)] = (sPE, pe_ctx)
                        if g + 1 < NG:
                            emit_kv_slab(c, g + 1, j // SLAB, ktv, vtv)
                if flushctr[0] % 14 == 0:
                    precast_pump(1)
                flushctr[0] += 1
                if grp == NK // 2 - 1:
                    ac_lnl = AC.add(
                        (lambda lb=lb: lambda sc: sc.activation(
                            MB[:, :], BNK[lb], AF.Ln))(),
                        waits=[(sPE, pe_L), free.get(("mb",))], inc=sAC)
                    free[("pb", lb)] = (sAC, ac_lnl)
                    ac_rl = AC.add(lambda sc: sc.activation(
                        RSD[:, :], MB[:, :], AF.Exp, scale=-1.0),
                        waits=[(sAC, ac_lnl), free.get(("rsd",))], inc=sAC)
                    free[("mb",)] = (sAC, ac_rl)
                    dv_c = DV.add(
                        (lambda h=h, cb=cb: lambda v: v.tensor_tensor(
                            tile(QC, h), BNK[cb], RSD[:, :], ALU.mult))(),
                        waits=[(sPE, pe_ctx), (sAC, ac_rl)], inc=sDV)
                    free[("pb", cb)] = (sDV, dv_c)
                    free[("rsd",)] = (sDV, dv_c)
                    dv_ctx[h] = dv_c

            for g in range(NG):
                for qh in range(4):
                    for grp in range(NK // 2):
                        gg, ac_e = emit_scores_group(g, qh, grp)
                        pending.append((g, qh, grp, gg, ac_e))
                        if len(pending) > 2:
                            flush_group(pending.pop(0))
                while pending:
                    flush_group(pending.pop(0))
            free[("ps4", 0)] = (sPE, sPE.val)

        def emit_WO(c):
            ctx_all = (sDV, dv_ctx[NH - 1])
            pe_last = None
            for i in range(DT):
                dghalf = []
                for hh in range(2):
                    wsem = sWS0 if hh == 0 else sWS1
                    dg_cb = AC.add(
                        (lambda i=i, hh=hh: lambda sc: sc.dma_start(
                            WB[:, hh * 2048:(hh + 1) * 2048],
                            wob_d[2 * i + hh]))(),
                        waits=[free.get(("wb", hh)), pco_done.get("o")],
                        inc=wsem, inc_amt=16)
                    dghalf.append((wsem, dg_cb))
                b = i % 2
                pe_o = None
                for j in range(DT):
                    w = []
                    if j == 0:
                        w = [dghalf[0], ctx_all, bw(b)]
                    elif j == 16:
                        w = [dghalf[1]]
                    pe_o = PE.add(
                        (lambda i=i, j=j, b=b: lambda t: t.matmul(
                            BNK[b],
                            WB[:, (0 if j < 16 else 2048) + (j % 16) * 128:
                                  (0 if j < 16 else 2048) + (j % 16) * 128 + 128],
                            tile(QC, j), start=(j == 0), stop=(j == DT - 1)))(),
                        waits=w, inc=(sPE if j in (15, DT - 1) else None))
                    if j == 15:
                        free[("wb", 0)] = (sPE, pe_o)
                free[("wb", 1)] = (sPE, pe_o)
                pe_last = pe_o
                dv_add = DV.add(
                    (lambda i=i, b=b: lambda v: v.tensor_tensor(
                        tile(X, i), BNK[b], tile(X, i), ALU.add))(),
                    waits=[(sPE, pe_o), free.get(("xrd", i))], inc=sDV)
                free[("pb", b)] = (sDV, dv_add)
                x_state[c][i] = (sDV, dv_add)
            free[("qc_all",)] = (sPE, pe_last)

        def emit_STORE(c):
            st_last = None
            for i in range(DT):
                st_last = SY.add(
                    (lambda i=i, c=c: lambda s: s.dma_start(
                        y_d.rearrange("(i p) t -> p i t", p=128)
                        [:, i, c * CH:(c + 1) * CH],
                        tile(X, i)))(),
                    waits=[x_state[c][i]], inc=sST, inc_amt=16)
            if c == 0:
                ld_last = None
                for i in range(DT):
                    ld_last = SY.add(
                        (lambda i=i: lambda s: s.dma_start(
                            tile(X, i),
                            x_d.rearrange("(i p) t -> p i t", p=128)
                            [:, i, CH:2 * CH]))(),
                        waits=[(sST, st_last)], inc=sXL, inc_amt=16)
                for i in range(DT):
                    x_state[1][i] = (sXL, ld_last)

        for c in range(NCH):
            if c > 0:
                SY.add(lambda s: s.dma_start(RPC[:, :], ropc_d[:, CH:2 * CH]),
                       waits=[free.get(("rope",))], inc=sRP, inc_amt=16)
                rv = SY.add(lambda s: s.dma_start(RPS[:, :], rops_d[:, CH:2 * CH]),
                            inc=sRP, inc_amt=16)
                rope_ready[c] = (sRP, rv)
            dv_xn = emit_LN(c)
            emit_FFN(c, dv_xn)
            cb_pre = []
            for hh in range(2):
                wsem = sWB0 if hh == 0 else sWB1
                cb_pre.append((wsem, GP.add(
                    (lambda hh=hh: lambda g: g.dma_start(
                        WB[:, hh * 2048:(hh + 1) * 2048]
                        .rearrange("p (j m) -> p j m", j=16),
                        wq_d.rearrange("(j p) m -> p j m", p=128)
                        [:, hh * 16:(hh + 1) * 16, 0:128]))(),
                    waits=[free.get(("wb", hh))], inc=wsem, inc_amt=16)))
            dv_xn = emit_LN(c)
            po_qt = emit_QT(c, dv_xn, cb_pre)
            if c == 0:
                pc_queue.extend([("o", b) for b in range(2 * DT)])
                pc_queue.extend([("q", b) for b in range(2 * DT)])
            emit_ATT(c, po_qt)
            while pc_queue:
                precast_pump(1)
            emit_WO(c)
            dv_xn = emit_LN(c)
            emit_FFN(c, dv_xn)
            emit_STORE(c)

        SY.add(None, waits=[(sST, sST.val)])

        @block.tensor
        def _(t):
            PE.replay(t)

        @block.scalar
        def _(sc):
            AC.replay(sc)

        @block.vector
        def _(v):
            DV.replay(v)

        @block.gpsimd
        def _(g):
            GP.replay(g)

        @block.sync
        def _(s):
            SY.replay(s)

    return nc


_CACHE = {}


def _get_program():
    if "nc" not in _CACHE:
        _CACHE["nc"] = build_program()
    return _CACHE["nc"]


def kernel(L0_attn_hidden_states, L0_attn_Wq, L0_attn_Wo, L0_attn_kv_cache,
           L0_attn_block_table, L0_ffn_A, L0_ffn_B, L0_ffn_Wr, L0_ffn_br):
    hs = np.ascontiguousarray(np.asarray(L0_attn_hidden_states, dtype=np.float32))
    wq = np.ascontiguousarray(np.asarray(L0_attn_Wq, dtype=np.float32))
    wo = np.ascontiguousarray(np.asarray(L0_attn_Wo, dtype=np.float32))
    kv = np.asarray(L0_attn_kv_cache, dtype=np.float32)
    btab = np.asarray(L0_attn_block_table, dtype=np.int64)
    A = np.asarray(L0_ffn_A, dtype=np.float32)
    B = np.asarray(L0_ffn_B, dtype=np.float32)
    Wr = np.asarray(L0_ffn_Wr, dtype=np.float32)
    br = np.asarray(L0_ffn_br, dtype=np.float32)

    blocks = kv[btab]
    ktm = np.ascontiguousarray(
        blocks[:, 0].transpose(1, 3, 0, 2).reshape(NG * 128, NKEY))
    vtm = np.ascontiguousarray(
        blocks[:, 1].transpose(1, 0, 2, 3).reshape(NG * NKEY, 128))
    atm = np.ascontiguousarray(A.reshape(128, D).T)
    btm = np.ascontiguousarray(B.transpose(0, 2, 1).reshape(128, D))
    wrtm = np.ascontiguousarray(Wr.T)
    brm = np.zeros((128, 1), np.float32); brm[:8, 0] = br

    rep = np.zeros((128, 128), np.float32)
    for h in range(8):
        rep[h, h * 16:(h + 1) * 16] = 1.0
    rpm = np.zeros((128, 128), np.float32)
    for m in range(64):
        rpm[m + 64, m] = -1.0
    for m in range(64, 128):
        rpm[m - 64, m] = 1.0
    invf = (10000.0 ** (-np.arange(64, dtype=np.float64) / 64.0))

    hs2 = hs.reshape(2 * 4096, D)
    in_maps = []
    for cid in range(NCORE):
        toks = cid * TOK + np.arange(TOK)
        pos = (toks % 4096).astype(np.float64)
        ang = pos[None, :] * invf[:, None]
        ang = np.concatenate([ang, ang], axis=0)
        ropc = np.cos(ang).astype(np.float32)
        rops = np.sin(ang).astype(np.float32)
        xc = np.ascontiguousarray(hs2[cid * TOK:(cid + 1) * TOK].T)
        in_maps.append({
            "x": xc, "kt": ktm, "vt": vtm, "wq": wq, "wo": wo,
            "at": atm, "bt": btm, "wrt": wrtm, "br": brm,
            "rep": rep, "rpm": rpm, "ropc": np.ascontiguousarray(ropc),
            "rops": np.ascontiguousarray(rops),
        })

    nc = _get_program()
    res = run_bass_kernel_spmd(nc, in_maps, core_ids=list(range(NCORE)))
    out = np.empty((2 * 4096, D), np.float32)
    for cid in range(NCORE):
        out[cid * TOK:(cid + 1) * TOK] = np.asarray(
            res.results[cid]["y"], dtype=np.float32).T
    return out.reshape(2, 4096, D)


# version 18
# speedup vs baseline: 1.0226x; 1.0226x over previous
import math
import numpy as np
import ml_dtypes
import concourse.bass as bass
import concourse.mybir as mybir
from concourse.bass_utils import run_bass_kernel_spmd

F32 = mybir.dt.float32
BF16 = mybir.dt.bfloat16
AF = mybir.ActivationFunctionType
ALU = mybir.AluOpType

NCORE = 8
D = 4096
TOK = 1024
CH = 512
NCH = TOK // CH
DT = D // 128
NH = 32
NG = 8
NK = 64
NKEY = 8192
SLAB = 16
NSLAB = NK // SLAB
NPS = 6
ISQ = 1.0 / math.sqrt(128.0)
SHIFT = 12.0
EPS = 1e-5


class Sem:
    def __init__(self, name):
        self.name = name
        self.val = 0
        self.h = None


class Stream:
    def __init__(self, name):
        self.name = name
        self.ops = []
        self.waited = {}

    def add(self, fn, waits=(), inc=None, inc_amt=1):
        real = []
        for w in waits:
            if w is None:
                continue
            s, v = w
            if v is None or v <= 0:
                continue
            if self.waited.get(s.name, -1) >= v:
                continue
            self.waited[s.name] = v
            real.append((s, v))
        post = None
        if inc is not None:
            inc.val += inc_amt
            post = inc.val
        self.ops.append((fn, real, inc, inc_amt))
        return post

    def replay(self, eng):
        for fn, waits, inc, amt in self.ops:
            for s, v in waits:
                eng.wait_ge(s.h, v)
            if fn is None:
                continue
            instr = fn(eng)
            if inc is not None:
                instr.then_inc(inc.h, amt)


def build_program():
    nc = bass.Bass("TRN2", target_bir_lowering=False)

    x_d = nc.dram_tensor("x", [D, TOK], F32, kind="ExternalInput").ap()
    kt_d = nc.dram_tensor("kt", [NG * 128, NKEY], F32, kind="ExternalInput").ap()
    vt_d = nc.dram_tensor("vt", [NG * NKEY, 128], F32, kind="ExternalInput").ap()
    wq_d = nc.dram_tensor("wq", [D, D], F32, kind="ExternalInput").ap()
    wo_d = nc.dram_tensor("wo", [D, D], F32, kind="ExternalInput").ap()
    at_d = nc.dram_tensor("at", [D, 128], F32, kind="ExternalInput").ap()
    bt_d = nc.dram_tensor("bt", [128, D], F32, kind="ExternalInput").ap()
    wrt_d = nc.dram_tensor("wrt", [D, 8], F32, kind="ExternalInput").ap()
    br_d = nc.dram_tensor("br", [128, 1], F32, kind="ExternalInput").ap()
    rep_d = nc.dram_tensor("rep", [128, 128], F32, kind="ExternalInput").ap()
    rpm_d = nc.dram_tensor("rpm", [128, 128], F32, kind="ExternalInput").ap()
    ropc_d = nc.dram_tensor("ropc", [128, TOK], F32, kind="ExternalInput").ap()
    rops_d = nc.dram_tensor("rops", [128, TOK], F32, kind="ExternalInput").ap()
    y_d = nc.dram_tensor("y", [D, TOK], F32, kind="ExternalOutput").ap()
    wqb_d = nc.dram_tensor("wqb", [2 * DT, 128, 2048], BF16).ap()
    wob_d = nc.dram_tensor("wob", [2 * DT, 128, 2048], BF16).ap()

    sPE, sPS, sAC, sDV, sPO = (Sem(n) for n in
        ("sPE", "sPS", "sAC", "sDV", "sPO"))
    sW0, sWB0, sWB1, sIN, sRP, sXL, sST = (Sem(n) for n in
        ("sW0", "sWB0", "sWB1", "sIN", "sRP", "sXL", "sST"))
    sKT = [Sem(f"sKT{i}") for i in range(NSLAB)]
    sVT = [Sem(f"sVT{i}") for i in range(NSLAB)]
    sPCI, sPCOq, sPCOo, sWS0, sWS1 = (Sem(n) for n in
        ("sPCI", "sPCOq", "sPCOo", "sWS0", "sWS1"))
    sSTb, sXLb = Sem("sSTb"), Sem("sXLb")
    DMAS = [sW0, sWB0, sWB1, sIN, sRP, sXL, sST,
            sPCI, sPCOq, sPCOo, sWS0, sWS1, sSTb, sXLb] + sKT + sVT

    PE, AC, DV, GP, SY = (Stream(n) for n in
        ("tensor", "scalar", "vector", "gpsimd", "sync"))

    from contextlib import ExitStack
    ctx = ExitStack()
    with ctx:
        def sb(name, shape, dt):
            return ctx.enter_context(nc.sbuf_tensor(name, shape, dt))

        X = sb("X", [128, DT * CH], F32)
        QC = sb("QC", [128, DT * CH], BF16)
        XN = sb("XN", [128, DT * CH], BF16)
        KT = sb("KT", [128, NKEY], BF16)
        VT = sb("VT", [128, NKEY], BF16)
        WB = sb("WB", [128, 4096], BF16)
        ATW = sb("ATW", [128, DT * 128], BF16)
        WRT = sb("WRT", [128, DT * 8], BF16)
        RPC = sb("RPC", [128, CH], F32)
        RPS = sb("RPS", [128, CH], F32)
        RPM = sb("RPM", [128, 128], BF16)
        REP = sb("REP", [128, 128], BF16)
        ONES = sb("ONES", [128, 128], BF16)
        OO8 = sb("OO8", [8, 128], BF16)
        BRc = sb("BRc", [128, 1], F32)
        CST = sb("CST", [128, 4], F32)
        MB = sb("MB", [128, CH], F32)
        RSD = sb("RSD", [128, CH], F32)
        MBB = sb("MBB", [128, CH], BF16)
        RSDB = sb("RSDB", [128, CH], BF16)
        G0 = sb("G0", [128, CH], F32)
        SQr = sb("SQr", [128, CH], BF16)
        TTr = sb("TTr", [128, 2 * CH], BF16)
        UR = sb("UR", [128, CH], F32)
        PS4 = sb("PS4", [128, NPS * CH], BF16)
        E8 = sb("E8", [8, CH], BF16)
        PCB = sb("PCB", [128, 2048], BF16)
        psA = ctx.enter_context(nc.psum_tensor("psA", [128, 1024], F32))
        psB = ctx.enter_context(nc.psum_tensor("psB", [128, 1024], F32))
        pc0 = ctx.enter_context(nc.psum_tensor("pc0", [128, 512], F32))
        pc1 = ctx.enter_context(nc.psum_tensor("pc1", [128, 512], F32))
        pl0 = ctx.enter_context(nc.psum_tensor("pl0", [128, 512], F32))
        pl1 = ctx.enter_context(nc.psum_tensor("pl1", [128, 512], F32))
        for s in [sPE, sPS, sAC, sDV, sPO] + DMAS:
            s.h = ctx.enter_context(nc.semaphore(s.name))
        block = ctx.enter_context(nc.Block())

        BNK = {0: psA[:, 0:512], 1: psA[:, 512:1024],
               6: psB[:, 0:512], 7: psB[:, 512:1024],
               2: pc0[:, :], 3: pc1[:, :], 4: pl0[:, :], 5: pl1[:, :]}

        free = {}

        def bw(b):
            return free.get(("pb", b))

        def tile(buf, i, w=CH):
            return buf[:, i * w:(i + 1) * w]

        DV.add(lambda v: v.memset(ONES[:, :], 1.0))
        DV.add(lambda v: v.memset(OO8[:, :], 0.125))
        DV.add(lambda v: v.memset(CST[:, 0:1], -SHIFT))
        DV.add(lambda v: v.memset(CST[:, 1:2], EPS))
        dv_const = DV.add(lambda v: v.memset(CST[:, 2:3], 1.0 / D), inc=sDV)

        SY.add(lambda s: s.dma_start(
            X[:, :].rearrange("p (i t) -> p i t", i=DT),
            x_d.rearrange("(i p) t -> p i t", p=128)[:, :, 0:CH]),
            inc=sIN, inc_amt=16)
        SY.add(lambda s: s.dma_start(RPC[:, :], ropc_d[:, 0:CH]), inc=sIN, inc_amt=16)
        SY.add(lambda s: s.dma_start(RPS[:, :], rops_d[:, 0:CH]), inc=sIN, inc_amt=16)
        ds_in0 = SY.add(lambda s: s.dma_start(BRc[:, :], br_d), inc=sIN, inc_amt=16)
        ds_x0 = ds_rope0 = ds_br = ds_in0

        GP.add(lambda g: g.dma_start(
            ATW[:, :].rearrange("p (j m) -> p j m", j=DT),
            at_d.rearrange("(j p) m -> p j m", p=128)), inc=sW0, inc_amt=16)
        GP.add(lambda g: g.dma_start(
            WRT[:, :].rearrange("p (j h) -> p j h", j=DT),
            wrt_d.rearrange("(j p) h -> p j h", p=128)), inc=sW0, inc_amt=16)
        GP.add(lambda g: g.dma_start(REP[:, :], rep_d), inc=sW0, inc_amt=16)
        dg_w0v = GP.add(lambda g: g.dma_start(RPM[:, :], rpm_d), inc=sW0, inc_amt=16)
        dg_w0 = None

        x_state = [[(sIN, ds_in0)] * DT, [None] * DT]
        rope_ready = [(sIN, ds_in0), None]
        dv_ctx = {}

        def emit_LN(c):
            pe_s1 = pe_s2 = None
            cast_v = {}
            s2_v = {}
            for i in range(DT):
                cast_v[i] = DV.add(
                    (lambda i=i: lambda v: v.tensor_copy(tile(XN, i), tile(X, i)))(),
                    waits=[x_state[c][i], free.get(("xn", i))], inc=sDV)
                free[("xrd", i)] = (sDV, cast_v[i])
                po_sq = GP.add(
                    (lambda i=i: lambda g: g.tensor_tensor(
                        SQr[:, :], tile(XN, i), tile(XN, i), ALU.mult))(),
                    waits=[(sDV, cast_v[i]), free.get(("sq",))], inc=sPO)
                w1 = [(sDV, cast_v[i])]
                if i == 0:
                    w1 += [bw(4), (sDV, dv_const)]
                pe_s1 = PE.add(
                    (lambda i=i: lambda t: t.matmul(
                        BNK[4], ONES[:, :], tile(XN, i),
                        start=(i == 0), stop=(i == DT - 1)))(),
                    waits=w1, inc=sPE)
                w2 = [(sPO, po_sq)]
                if i == 0:
                    w2 += [bw(5)]
                pe_s2 = PE.add(
                    (lambda i=i: lambda t: t.matmul(
                        BNK[5], ONES[:, :], SQr[:, :],
                        start=(i == 0), stop=(i == DT - 1)))(),
                    waits=w2, inc=sPE)
                s2_v[i] = pe_s2
                free[("sq",)] = (sPE, pe_s2)
            dv_m = DV.add(lambda v: v.tensor_scalar(
                MB[:, :], BNK[4], CST[:, 2:3], None, ALU.mult),
                waits=[(sPE, pe_s1), free.get(("mb",))], inc=sDV)
            dv_mb = DV.add(lambda v: v.tensor_copy(MBB[:, :], MB[:, :]),
                           waits=[(sDV, dv_m), free.get(("mbb",)),
                                  free.get(("mbb2",))], inc=sDV)
            dv_t2 = DV.add(lambda v: v.tensor_tensor(
                G0[:, :], MB[:, :], MB[:, :], ALU.mult),
                waits=[(sDV, dv_m), free.get(("g0",))], inc=sDV)
            dv_v = DV.add(lambda v: v.scalar_tensor_tensor(
                RSD[:, :], BNK[5], CST[:, 2:3], G0[:, :], ALU.mult, ALU.subtract),
                waits=[(sPE, pe_s2), (sDV, dv_t2), free.get(("rsd",))], inc=sDV)
            free[("pb", 4)] = (sDV, dv_m)
            free[("pb", 5)] = (sDV, dv_v)
            free[("mb",)] = (sDV, dv_t2)
            ac_ln = AC.add(lambda sc: sc.activation(
                G0[:, :], RSD[:, :], AF.Ln, bias=CST[:, 1:2]),
                waits=[(sDV, dv_v)], inc=sAC)
            free[("rsd",)] = (sAC, ac_ln)
            ac_r = AC.add(lambda sc: sc.activation(
                RSDB[:, :], G0[:, :], AF.Exp, scale=-0.5),
                waits=[(sAC, ac_ln), free.get(("rsdb",)),
                       free.get(("rsdb2",))], inc=sAC)
            free[("g0",)] = (sAC, ac_r)
            dv_xn = {}
            last_pool = last_dv = None
            for i in range(DT):
                onpool = (i % 2 == 0)
                E = GP if onpool else DV
                sem = sPO if onpool else sDV
                sl = 0 if onpool else 1
                wsub = [(sPE, s2_v[i]), (sDV, dv_mb), free.get(("tts", sl))]
                v_sub = E.add(
                    (lambda i=i, sl=sl: lambda e: e.tensor_tensor(
                        tile(TTr, sl), tile(XN, i), MBB[:, :], ALU.subtract))(),
                    waits=wsub, inc=sem)
                v_mul = E.add(
                    (lambda i=i, sl=sl: lambda e: e.tensor_tensor(
                        tile(XN, i), tile(TTr, sl), RSDB[:, :], ALU.mult))(),
                    waits=[(sem, v_sub), (sAC, ac_r)], inc=sem)
                free[("tts", sl)] = (sem, v_mul)
                dv_xn[i] = (sem, v_mul)
                if onpool:
                    last_pool = v_mul
                else:
                    last_dv = v_mul
            free[("mbb",)] = (sPO, last_pool)
            free[("mbb2",)] = (sDV, last_dv)
            free[("rsdb",)] = (sPO, last_pool)
            free[("rsdb2",)] = (sDV, last_dv)
            return dv_xn

        def emit_FFN(c, dv_xn):
            pe_lg = None
            for j in range(DT):
                w = [dv_xn[j]]
                if j == 0:
                    w += [(sW0, dg_w0v), bw(2)]
                pe_lg = PE.add(
                    (lambda j=j: lambda t: t.matmul(
                        BNK[2][:8, :], WRT[:, j * 8:(j + 1) * 8], tile(XN, j),
                        start=(j == 0), stop=(j == DT - 1)))(),
                    waits=w, inc=sPE)
                free[("xn", j)] = (sPE, pe_lg)
            ac_e = AC.add(lambda sc: sc.activation(
                E8[:, :], BNK[2][:8, :], AF.Exp, bias=BRc[:8, :]),
                waits=[(sPE, pe_lg), (sIN, ds_in0), free.get(("e8",))], inc=sAC)
            free[("pb", 2)] = (sAC, ac_e)
            pe_s8 = PE.add(lambda t: t.matmul(
                BNK[4], OO8[:, :], E8[:, :], start=True, stop=True),
                waits=[(sAC, ac_e), bw(4)], inc=sPE)
            pe_eb = PE.add(lambda t: t.matmul(
                BNK[5], REP[:8, :], E8[:, :], start=True, stop=True),
                waits=[bw(5)], inc=sPE)
            free[("e8",)] = (sPE, pe_eb)
            pe_low = None
            for j in range(DT):
                w = [dv_xn[j]] + ([bw(3)] if j == 0 else [])
                pe_low = PE.add(
                    (lambda j=j: lambda t: t.matmul(
                        BNK[3], ATW[:, j * 128:(j + 1) * 128], tile(XN, j),
                        start=(j == 0), stop=(j == DT - 1)))(),
                    waits=w, inc=sPE)
                free[("xn", j)] = (sPE, pe_low)
            dv_rb = DV.add(lambda v: v.reciprocal(RSD[:, :], BNK[4]),
                           waits=[(sPE, pe_s8), free.get(("rsd",))], inc=sDV)
            free[("pb", 4)] = (sDV, dv_rb)
            dv_t1 = DV.add(lambda v: v.tensor_tensor(
                G0[:, :], BNK[5], RSD[:, :], ALU.mult),
                waits=[(sPE, pe_eb), (sDV, dv_rb), free.get(("g0",))], inc=sDV)
            free[("pb", 5)] = (sDV, dv_t1)
            free[("rsd",)] = (sDV, dv_t1)
            lowp = PS4[:, 0:CH]
            dv_lp = DV.add(lambda v: v.tensor_tensor(
                lowp, BNK[3], G0[:, :], ALU.mult),
                waits=[(sPE, pe_low), (sDV, dv_t1), free.get(("ps4", 0))], inc=sDV)
            free[("pb", 3)] = (sDV, dv_lp)
            free[("g0",)] = (sDV, dv_lp)
            bthalf = []
            for hh in range(2):
                wsem = sWB0 if hh == 0 else sWB1
                dg_bt = GP.add(
                    (lambda hh=hh: lambda g: g.dma_start(
                        WB[:, hh * 2048:(hh + 1) * 2048],
                        bt_d[:, hh * 2048:(hh + 1) * 2048]))(),
                    waits=[free.get(("wb", hh))], inc=wsem, inc_amt=16)
                bthalf.append((wsem, dg_bt))
            obank = [0, 1, 6, 7]
            pe_half = [None, None]
            pe_last = None
            for i in range(DT):
                b = obank[i % 4]
                hh = 0 if i < 16 else 1
                w = [(sDV, dv_lp), bw(b)]
                if i in (0, 16):
                    w.append(bthalf[hh])
                pe_o = PE.add(
                    (lambda i=i, b=b, hh=hh: lambda t: t.matmul(
                        BNK[b],
                        WB[:, hh * 2048 + (i % 16) * 128:
                              hh * 2048 + (i % 16) * 128 + 128],
                        lowp, start=True, stop=True))(),
                    waits=w, inc=sPE)
                pe_half[hh] = pe_o
                pe_last = pe_o
                dv_add = DV.add(
                    (lambda i=i, b=b: lambda v: v.tensor_tensor(
                        tile(X, i), BNK[b], tile(X, i), ALU.add))(),
                    waits=[(sPE, pe_o), free.get(("xrd", i))], inc=sDV)
                free[("pb", b)] = (sDV, dv_add)
                x_state[c][i] = (sDV, dv_add)
            free[("ps4", 0)] = (sPE, pe_last)
            free[("wb", 0)] = (sPE, pe_half[0])
            free[("wb", 1)] = (sPE, pe_half[1])

        def emit_QT(c, dv_xn, cb_pre):
            po_qt = {}
            dv_qe = {}
            pe_qv = {}

            def rope_tail(i):
                rb_ = 6 + (i % 2)
                pe_rot = PE.add(
                    (lambda i=i, rb_=rb_: lambda t: t.matmul(
                        BNK[rb_], RPM[:, :], tile(QC, i),
                        start=True, stop=True))(),
                    waits=[(sDV, dv_qe[i]), bw(rb_), (sW0, dg_w0v)], inc=sPE)
                dv_t = DV.add(
                    (lambda i=i: lambda v: v.tensor_tensor(
                        G0[:, :], tile(QC, i), RPC[:, :], ALU.mult))(),
                    waits=[(sDV, dv_qe[i]), rope_ready[c], free.get(("g0",))],
                    inc=sDV)
                dv_u = DV.add(
                    (lambda rb_=rb_: lambda v: v.tensor_tensor(
                        UR[:, :], BNK[rb_], RPS[:, :], ALU.mult))(),
                    waits=[(sPE, pe_rot), free.get(("ur",))], inc=sDV)
                free[("pb", rb_)] = (sDV, dv_u)
                po_qt[i] = DV.add(
                    (lambda i=i: lambda v: v.tensor_tensor(
                        tile(QC, i), G0[:, :], UR[:, :], ALU.add))(),
                    waits=[(sDV, dv_t), (sDV, dv_u)], inc=sDV)
                free[("ur",)] = (sDV, po_qt[i])
                free[("g0",)] = (sDV, po_qt[i])

            for i in range(DT):
                if i == 0 and cb_pre is not None:
                    dghalf = cb_pre
                elif c > 0:
                    dghalf = []
                    for hh in range(2):
                        wsem = sWS0 if hh == 0 else sWS1
                        dg_cb = AC.add(
                            (lambda i=i, hh=hh: lambda sc: sc.dma_start(
                                WB[:, hh * 2048:(hh + 1) * 2048],
                                wqb_d[2 * i + hh]))(),
                            waits=[free.get(("wb", hh)), pco_done.get("q")],
                            inc=wsem, inc_amt=16)
                        dghalf.append((wsem, dg_cb))
                else:
                    dghalf = []
                    for hh in range(2):
                        wsem = sWB0 if hh == 0 else sWB1
                        dg_cb = GP.add(
                            (lambda i=i, hh=hh: lambda g: g.dma_start(
                                WB[:, hh * 2048:(hh + 1) * 2048]
                                .rearrange("p (j m) -> p j m", j=16),
                                wq_d.rearrange("(j p) m -> p j m", p=128)
                                [:, hh * 16:(hh + 1) * 16, i * 128:(i + 1) * 128]))(),
                            waits=[free.get(("wb", hh))], inc=wsem, inc_amt=16)
                        dghalf.append((wsem, dg_cb))
                b = i % 2
                pe_q = None
                for j in range(DT):
                    w = [dv_xn[j]]
                    if j == 0:
                        w += [dghalf[0], bw(b)]
                    elif j == 16:
                        w += [dghalf[1]]
                    pe_q = PE.add(
                        (lambda i=i, j=j, b=b: lambda t: t.matmul(
                            BNK[b],
                            WB[:, (0 if j < 16 else 2048) + (j % 16) * 128:
                                  (0 if j < 16 else 2048) + (j % 16) * 128 + 128],
                            tile(XN, j), start=(j == 0), stop=(j == DT - 1)))(),
                        waits=w, inc=(sPE if j in (15, DT - 1) else None))
                    if j == 15:
                        free[("wb", 0)] = (sPE, pe_q)
                free[("wb", 1)] = (sPE, pe_q)
                pe_qv[i] = pe_q
                dv_qe[i] = DV.add(
                    (lambda i=i, b=b: lambda v: v.tensor_copy(
                        tile(QC, i), BNK[b]))(),
                    waits=[(sPE, pe_q), free.get(("qc_all",))], inc=sDV)
                free[("pb", b)] = (sDV, dv_qe[i])
                if i >= 1:
                    rope_tail(i - 1)
            rope_tail(DT - 1)
            free[("rope",)] = (sDV, po_qt[DT - 1])
            for j in range(DT):
                free[("xn", j)] = (sPE, pe_qv[DT - 1])
            return po_qt

        pc_queue = []
        pco_done = {}

        def precast_pump(n=1):
            for _ in range(n):
                if not pc_queue:
                    return
                which, blk = pc_queue.pop(0)
                i, hh = blk // 2, blk % 2
                w_d = wo_d if which == "o" else wq_d
                dst_d = wob_d if which == "o" else wqb_d
                osem = sPCOo if which == "o" else sPCOq
                GP.add(
                    (lambda i=i, hh=hh, w_d=w_d: lambda g: g.dma_start(
                        PCB[:, :].rearrange("p (j m) -> p j m", j=16),
                        w_d.rearrange("(j p) m -> p j m", p=128)
                        [:, hh * 16:(hh + 1) * 16, i * 128:(i + 1) * 128]))(),
                    waits=[free.get(("pcb",))], inc=sPCI, inc_amt=16)
                v = SY.add(
                    (lambda blk=blk, dst_d=dst_d: lambda s: s.dma_start(
                        dst_d[blk], PCB[:, :]))(),
                    waits=[(sPCI, sPCI.val)], inc=osem, inc_amt=16)
                free[("pcb",)] = (osem, v)
                pco_done[which] = (osem, v)

        def emit_kv_slab(c, g, s, ktv, vtv):
            v = GP.add(
                (lambda g=g, s=s: lambda gp: gp.dma_start(
                    KT[:, s * SLAB * 128:(s + 1) * SLAB * 128],
                    kt_d[g * 128:(g + 1) * 128,
                         s * SLAB * 128:(s + 1) * SLAB * 128]))(),
                waits=[free.get(("kvslab", s))], inc=sKT[s], inc_amt=16)
            ktv[(g, s)] = (sKT[s], v)
            v = GP.add(
                (lambda g=g, s=s: lambda gp: gp.dma_start(
                    VT[:, s * SLAB * 128:(s + 1) * SLAB * 128]
                    .rearrange("p (j d) -> p j d", j=SLAB),
                    vt_d.rearrange("(j p) d -> p j d", p=128)
                    [:, g * NK + s * SLAB:g * NK + (s + 1) * SLAB, :]))(),
                waits=[free.get(("kvslab", s))], inc=sVT[s], inc_amt=16)
            vtv[(g, s)] = (sVT[s], v)

        def emit_kv_dma(c, g, ktv, vtv):
            for s in range(NSLAB):
                emit_kv_slab(c, g, s, ktv, vtv)

        flushctr = [0]

        def emit_ATT(c, po_qt):
            ktv, vtv = {}, {}
            emit_kv_dma(c, 0, ktv, vtv)
            ggl = [0]
            pending = []

            def emit_scores_group(g, qh, grp):
                h = g * 4 + qh
                gg = ggl[0]
                ggl[0] += 1
                lo = 0 if (gg % 2 == 0) else 6
                ps_val = None
                for t in range(2):
                    j = 2 * grp + t
                    w = [ktv[(g, j // SLAB)]]
                    if t == 0:
                        w += [bw(lo), bw(lo + 1), (sDV, po_qt[h])]
                    ps_val = PE.add(
                        (lambda j=j, b=lo + t, h=h: lambda tt: tt.matmul(
                            BNK[b], KT[:, j * 128:(j + 1) * 128],
                            tile(QC, h), start=True, stop=True))(),
                        waits=w, inc=(sPS if t == 1 else None))
                slot = (2 * gg) % NPS
                big = psA if (gg % 2 == 0) else psB
                ac_e = AC.add(
                    (lambda slot=slot, big=big: lambda sc: sc.activation(
                        PS4[:, slot * CH:(slot + 2) * CH], big[:, :], AF.Exp,
                        bias=CST[:, 0:1], scale=ISQ))(),
                    waits=[(sPS, ps_val)], inc=sAC)
                free[("pb", lo)] = (sAC, ac_e)
                free[("pb", lo + 1)] = (sAC, ac_e)
                return gg, ac_e

            def flush_group(item):
                g, qh, grp, gg, ac_e = item
                h = g * 4 + qh
                cb = 2 + (h % 2)
                lb = 4 + (h % 2)
                slot = (2 * gg) % NPS
                pe_L = pe_ctx = None
                for t in range(2):
                    j = 2 * grp + t
                    wl = [(sAC, ac_e)]
                    if j == 0:
                        wl.append(bw(lb))
                    pe_L = PE.add(
                        (lambda lb=lb, slot=slot, t=t, j=j: lambda tt: tt.matmul(
                            BNK[lb], ONES[:, :],
                            PS4[:, (slot + t) * CH:(slot + t + 1) * CH],
                            start=(j == 0), stop=(j == NK - 1)))(),
                        waits=wl, inc=(sPE if j == NK - 1 else None))
                    wc = [vtv[(g, j // SLAB)]]
                    if j == 0:
                        wc.append(bw(cb))
                    need_inc = (t == 1) and (
                        (qh == 3 and (j % SLAB) == SLAB - 1)
                        or grp == NK // 2 - 1)
                    pe_ctx = PE.add(
                        (lambda g=g, j=j, slot=slot, t=t, cb=cb: lambda tt: tt.matmul(
                            BNK[cb], VT[:, j * 128:(j + 1) * 128],
                            PS4[:, (slot + t) * CH:(slot + t + 1) * CH],
                            start=(j == 0), stop=(j == NK - 1)))(),
                        waits=wc, inc=(sPE if need_inc else None))
                    if qh == 3 and t == 1 and (j % SLAB) == SLAB - 1:
                        free[("kvslab", j // SLAB)] = (sPE, pe_ctx)
                        if g + 1 < NG:
                            emit_kv_slab(c, g + 1, j // SLAB, ktv, vtv)
                if flushctr[0] % 11 == 0:
                    precast_pump(1)
                flushctr[0] += 1
                if grp == NK // 2 - 1:
                    ac_lnl = AC.add(
                        (lambda lb=lb: lambda sc: sc.activation(
                            MB[:, :], BNK[lb], AF.Ln))(),
                        waits=[(sPE, pe_L), free.get(("mb",))], inc=sAC)
                    free[("pb", lb)] = (sAC, ac_lnl)
                    ac_rl = AC.add(lambda sc: sc.activation(
                        RSD[:, :], MB[:, :], AF.Exp, scale=-1.0),
                        waits=[(sAC, ac_lnl), free.get(("rsd",))], inc=sAC)
                    free[("mb",)] = (sAC, ac_rl)
                    dv_c = DV.add(
                        (lambda h=h, cb=cb: lambda v: v.tensor_tensor(
                            tile(QC, h), BNK[cb], RSD[:, :], ALU.mult))(),
                        waits=[(sPE, pe_ctx), (sAC, ac_rl)], inc=sDV)
                    free[("pb", cb)] = (sDV, dv_c)
                    free[("rsd",)] = (sDV, dv_c)
                    dv_ctx[h] = dv_c

            for g in range(NG):
                for qh in range(4):
                    for grp in range(NK // 2):
                        gg, ac_e = emit_scores_group(g, qh, grp)
                        pending.append((g, qh, grp, gg, ac_e))
                        if len(pending) > 2:
                            flush_group(pending.pop(0))
                while pending:
                    flush_group(pending.pop(0))
            free[("ps4", 0)] = (sPE, sPE.val)

        def emit_WO(c):
            ctx_all = (sDV, dv_ctx[NH - 1])
            pe_last = None
            for i in range(DT):
                dghalf = []
                for hh in range(2):
                    wsem = sWS0 if hh == 0 else sWS1
                    dg_cb = AC.add(
                        (lambda i=i, hh=hh: lambda sc: sc.dma_start(
                            WB[:, hh * 2048:(hh + 1) * 2048],
                            wob_d[2 * i + hh]))(),
                        waits=[free.get(("wb", hh)), pco_done.get("o")],
                        inc=wsem, inc_amt=16)
                    dghalf.append((wsem, dg_cb))
                b = i % 2
                pe_o = None
                for j in range(DT):
                    w = []
                    if j == 0:
                        w = [dghalf[0], ctx_all, bw(b)]
                    elif j == 16:
                        w = [dghalf[1]]
                    pe_o = PE.add(
                        (lambda i=i, j=j, b=b: lambda t: t.matmul(
                            BNK[b],
                            WB[:, (0 if j < 16 else 2048) + (j % 16) * 128:
                                  (0 if j < 16 else 2048) + (j % 16) * 128 + 128],
                            tile(QC, j), start=(j == 0), stop=(j == DT - 1)))(),
                        waits=w, inc=(sPE if j in (15, DT - 1) else None))
                    if j == 15:
                        free[("wb", 0)] = (sPE, pe_o)
                free[("wb", 1)] = (sPE, pe_o)
                pe_last = pe_o
                dv_add = DV.add(
                    (lambda i=i, b=b: lambda v: v.tensor_tensor(
                        tile(X, i), BNK[b], tile(X, i), ALU.add))(),
                    waits=[(sPE, pe_o), free.get(("xrd", i))], inc=sDV)
                free[("pb", b)] = (sDV, dv_add)
                x_state[c][i] = (sDV, dv_add)
            free[("qc_all",)] = (sPE, pe_last)

        def emit_STORE(c):
            sta = stb = None
            for i in range(DT):
                ch = sST if (c == 1 or i < DT // 2) else sSTb
                v = SY.add(
                    (lambda i=i, c=c: lambda s: s.dma_start(
                        y_d.rearrange("(i p) t -> p i t", p=128)
                        [:, i, c * CH:(c + 1) * CH],
                        tile(X, i)))(),
                    waits=[x_state[c][i]], inc=ch, inc_amt=16)
                if ch is sST:
                    sta = v
                else:
                    stb = v
            if c == 0:
                lda = ldb = None
                for i in range(DT):
                    ch = sXL if i < DT // 2 else sXLb
                    v = SY.add(
                        (lambda i=i: lambda s: s.dma_start(
                            tile(X, i),
                            x_d.rearrange("(i p) t -> p i t", p=128)
                            [:, i, CH:2 * CH]))(),
                        waits=[(sST, sta) if i < DT // 2 else (sSTb, stb)],
                        inc=ch, inc_amt=16)
                    if ch is sXL:
                        lda = v
                    else:
                        ldb = v
                for i in range(DT):
                    x_state[1][i] = (sXL, lda) if i < DT // 2 else (sXLb, ldb)

        for c in range(NCH):
            if c > 0:
                SY.add(lambda s: s.dma_start(RPC[:, :], ropc_d[:, CH:2 * CH]),
                       waits=[free.get(("rope",))], inc=sRP, inc_amt=16)
                rv = SY.add(lambda s: s.dma_start(RPS[:, :], rops_d[:, CH:2 * CH]),
                            inc=sRP, inc_amt=16)
                rope_ready[c] = (sRP, rv)
            dv_xn = emit_LN(c)
            emit_FFN(c, dv_xn)
            cb_pre = []
            for hh in range(2):
                wsem = sWB0 if hh == 0 else sWB1
                cb_pre.append((wsem, GP.add(
                    (lambda hh=hh: lambda g: g.dma_start(
                        WB[:, hh * 2048:(hh + 1) * 2048]
                        .rearrange("p (j m) -> p j m", j=16),
                        wq_d.rearrange("(j p) m -> p j m", p=128)
                        [:, hh * 16:(hh + 1) * 16, 0:128]))(),
                    waits=[free.get(("wb", hh))], inc=wsem, inc_amt=16)))
            dv_xn = emit_LN(c)
            po_qt = emit_QT(c, dv_xn, cb_pre)
            if c == 0:
                pc_queue.extend([("o", b) for b in range(2 * DT)])
                pc_queue.extend([("q", b) for b in range(2 * DT)])
            emit_ATT(c, po_qt)
            while pc_queue:
                precast_pump(1)
            emit_WO(c)
            dv_xn = emit_LN(c)
            emit_FFN(c, dv_xn)
            emit_STORE(c)

        SY.add(None, waits=[(sST, sST.val), (sSTb, sSTb.val)])

        @block.tensor
        def _(t):
            PE.replay(t)

        @block.scalar
        def _(sc):
            AC.replay(sc)

        @block.vector
        def _(v):
            DV.replay(v)

        @block.gpsimd
        def _(g):
            GP.replay(g)

        @block.sync
        def _(s):
            SY.replay(s)

    return nc


_CACHE = {}


def _get_program():
    if "nc" not in _CACHE:
        _CACHE["nc"] = build_program()
    return _CACHE["nc"]


def kernel(L0_attn_hidden_states, L0_attn_Wq, L0_attn_Wo, L0_attn_kv_cache,
           L0_attn_block_table, L0_ffn_A, L0_ffn_B, L0_ffn_Wr, L0_ffn_br):
    hs = np.ascontiguousarray(np.asarray(L0_attn_hidden_states, dtype=np.float32))
    wq = np.ascontiguousarray(np.asarray(L0_attn_Wq, dtype=np.float32))
    wo = np.ascontiguousarray(np.asarray(L0_attn_Wo, dtype=np.float32))
    kv = np.asarray(L0_attn_kv_cache, dtype=np.float32)
    btab = np.asarray(L0_attn_block_table, dtype=np.int64)
    A = np.asarray(L0_ffn_A, dtype=np.float32)
    B = np.asarray(L0_ffn_B, dtype=np.float32)
    Wr = np.asarray(L0_ffn_Wr, dtype=np.float32)
    br = np.asarray(L0_ffn_br, dtype=np.float32)

    blocks = kv[btab]
    ktm = np.ascontiguousarray(
        blocks[:, 0].transpose(1, 3, 0, 2).reshape(NG * 128, NKEY))
    vtm = np.ascontiguousarray(
        blocks[:, 1].transpose(1, 0, 2, 3).reshape(NG * NKEY, 128))
    atm = np.ascontiguousarray(A.reshape(128, D).T)
    btm = np.ascontiguousarray(B.transpose(0, 2, 1).reshape(128, D))
    wrtm = np.ascontiguousarray(Wr.T)
    brm = np.zeros((128, 1), np.float32); brm[:8, 0] = br

    rep = np.zeros((128, 128), np.float32)
    for h in range(8):
        rep[h, h * 16:(h + 1) * 16] = 1.0
    rpm = np.zeros((128, 128), np.float32)
    for m in range(64):
        rpm[m + 64, m] = -1.0
    for m in range(64, 128):
        rpm[m - 64, m] = 1.0
    invf = (10000.0 ** (-np.arange(64, dtype=np.float64) / 64.0))

    hs2 = hs.reshape(2 * 4096, D)
    in_maps = []
    for cid in range(NCORE):
        toks = cid * TOK + np.arange(TOK)
        pos = (toks % 4096).astype(np.float64)
        ang = pos[None, :] * invf[:, None]
        ang = np.concatenate([ang, ang], axis=0)
        ropc = np.cos(ang).astype(np.float32)
        rops = np.sin(ang).astype(np.float32)
        xc = np.ascontiguousarray(hs2[cid * TOK:(cid + 1) * TOK].T)
        in_maps.append({
            "x": xc, "kt": ktm, "vt": vtm, "wq": wq, "wo": wo,
            "at": atm, "bt": btm, "wrt": wrtm, "br": brm,
            "rep": rep, "rpm": rpm, "ropc": np.ascontiguousarray(ropc),
            "rops": np.ascontiguousarray(rops),
        })

    nc = _get_program()
    res = run_bass_kernel_spmd(nc, in_maps, core_ids=list(range(NCORE)))
    out = np.empty((2 * 4096, D), np.float32)
    for cid in range(NCORE):
        out[cid * TOK:(cid + 1) * TOK] = np.asarray(
            res.results[cid]["y"], dtype=np.float32).T
    return out.reshape(2, 4096, D)


# version 19
# speedup vs baseline: 1.0317x; 1.0089x over previous
import math
import numpy as np
import ml_dtypes
import concourse.bass as bass
import concourse.mybir as mybir
from concourse.bass_utils import run_bass_kernel_spmd

F32 = mybir.dt.float32
BF16 = mybir.dt.bfloat16
AF = mybir.ActivationFunctionType
ALU = mybir.AluOpType

NCORE = 8
D = 4096
TOK = 1024
CH = 512
NCH = TOK // CH
DT = D // 128
NH = 32
NG = 8
NK = 64
NKEY = 8192
SLAB = 16
NSLAB = NK // SLAB
NPS = 6
ISQ = 1.0 / math.sqrt(128.0)
SHIFT = 12.0
EPS = 1e-5


class Sem:
    def __init__(self, name):
        self.name = name
        self.val = 0
        self.h = None


class Stream:
    def __init__(self, name):
        self.name = name
        self.ops = []
        self.waited = {}

    def add(self, fn, waits=(), inc=None, inc_amt=1):
        real = []
        for w in waits:
            if w is None:
                continue
            s, v = w
            if v is None or v <= 0:
                continue
            if self.waited.get(s.name, -1) >= v:
                continue
            self.waited[s.name] = v
            real.append((s, v))
        post = None
        if inc is not None:
            inc.val += inc_amt
            post = inc.val
        self.ops.append((fn, real, inc, inc_amt))
        return post

    def replay(self, eng):
        for fn, waits, inc, amt in self.ops:
            for s, v in waits:
                eng.wait_ge(s.h, v)
            if fn is None:
                continue
            instr = fn(eng)
            if inc is not None:
                instr.then_inc(inc.h, amt)


def build_program():
    nc = bass.Bass("TRN2", target_bir_lowering=False)

    x_d = nc.dram_tensor("x", [D, TOK], F32, kind="ExternalInput").ap()
    kt_d = nc.dram_tensor("kt", [NG * 128, NKEY], F32, kind="ExternalInput").ap()
    vt_d = nc.dram_tensor("vt", [NG * NKEY, 128], F32, kind="ExternalInput").ap()
    wq_d = nc.dram_tensor("wq", [D, D], F32, kind="ExternalInput").ap()
    wo_d = nc.dram_tensor("wo", [D, D], F32, kind="ExternalInput").ap()
    at_d = nc.dram_tensor("at", [D, 128], F32, kind="ExternalInput").ap()
    bt_d = nc.dram_tensor("bt", [128, D], F32, kind="ExternalInput").ap()
    wrt_d = nc.dram_tensor("wrt", [D, 8], F32, kind="ExternalInput").ap()
    br_d = nc.dram_tensor("br", [128, 1], F32, kind="ExternalInput").ap()
    rep_d = nc.dram_tensor("rep", [128, 128], F32, kind="ExternalInput").ap()
    rpm_d = nc.dram_tensor("rpm", [128, 128], F32, kind="ExternalInput").ap()
    ropc_d = nc.dram_tensor("ropc", [128, TOK], F32, kind="ExternalInput").ap()
    rops_d = nc.dram_tensor("rops", [128, TOK], F32, kind="ExternalInput").ap()
    y_d = nc.dram_tensor("y", [D, TOK], F32, kind="ExternalOutput").ap()
    wqb_d = nc.dram_tensor("wqb", [2 * DT, 128, 2048], BF16).ap()
    wob_d = nc.dram_tensor("wob", [2 * DT, 128, 2048], BF16).ap()

    sPE, sPS, sAC, sDV, sPO = (Sem(n) for n in
        ("sPE", "sPS", "sAC", "sDV", "sPO"))
    sW0, sWB0, sWB1, sIN, sRP, sXL, sST = (Sem(n) for n in
        ("sW0", "sWB0", "sWB1", "sIN", "sRP", "sXL", "sST"))
    sKT = [Sem(f"sKT{i}") for i in range(NSLAB)]
    sVT = [Sem(f"sVT{i}") for i in range(NSLAB)]
    sPCI, sPCOq, sPCOo, sWS0, sWS1 = (Sem(n) for n in
        ("sPCI", "sPCOq", "sPCOo", "sWS0", "sWS1"))
    sSTb, sXLb = Sem("sSTb"), Sem("sXLb")
    DMAS = [sW0, sWB0, sWB1, sIN, sRP, sXL, sST,
            sPCI, sPCOq, sPCOo, sWS0, sWS1, sSTb, sXLb] + sKT + sVT

    PE, AC, DV, GP, SY = (Stream(n) for n in
        ("tensor", "scalar", "vector", "gpsimd", "sync"))

    from contextlib import ExitStack
    ctx = ExitStack()
    with ctx:
        def sb(name, shape, dt):
            return ctx.enter_context(nc.sbuf_tensor(name, shape, dt))

        X = sb("X", [128, DT * CH], F32)
        QC = sb("QC", [128, DT * CH], BF16)
        XN = sb("XN", [128, DT * CH], BF16)
        KT = sb("KT", [128, NKEY], BF16)
        VT = sb("VT", [128, NKEY], BF16)
        WB = sb("WB", [128, 4096], BF16)
        ATW = sb("ATW", [128, DT * 128], BF16)
        WRT = sb("WRT", [128, DT * 8], BF16)
        RPC = sb("RPC", [128, CH], F32)
        RPS = sb("RPS", [128, CH], F32)
        RPM = sb("RPM", [128, 128], BF16)
        REP = sb("REP", [128, 128], BF16)
        ONES = sb("ONES", [128, 128], BF16)
        OO8 = sb("OO8", [8, 128], BF16)
        BRc = sb("BRc", [128, 1], F32)
        CST = sb("CST", [128, 4], F32)
        MB = sb("MB", [128, CH], F32)
        RSD = sb("RSD", [128, CH], F32)
        MBB = sb("MBB", [128, CH], BF16)
        RSDB = sb("RSDB", [128, CH], BF16)
        G0 = sb("G0", [128, CH], F32)
        SQr = sb("SQr", [128, CH], BF16)
        TTr = sb("TTr", [128, 2 * CH], BF16)
        UR = sb("UR", [128, CH], F32)
        PS4 = sb("PS4", [128, NPS * CH], BF16)
        E8 = sb("E8", [8, CH], BF16)
        PCB = sb("PCB", [128, 2048], BF16)
        psA = ctx.enter_context(nc.psum_tensor("psA", [128, 1024], F32))
        psB = ctx.enter_context(nc.psum_tensor("psB", [128, 1024], F32))
        pc0 = ctx.enter_context(nc.psum_tensor("pc0", [128, 512], F32))
        pc1 = ctx.enter_context(nc.psum_tensor("pc1", [128, 512], F32))
        pl0 = ctx.enter_context(nc.psum_tensor("pl0", [128, 512], F32))
        pl1 = ctx.enter_context(nc.psum_tensor("pl1", [128, 512], F32))
        for s in [sPE, sPS, sAC, sDV, sPO] + DMAS:
            s.h = ctx.enter_context(nc.semaphore(s.name))
        block = ctx.enter_context(nc.Block())

        BNK = {0: psA[:, 0:512], 1: psA[:, 512:1024],
               6: psB[:, 0:512], 7: psB[:, 512:1024],
               2: pc0[:, :], 3: pc1[:, :], 4: pl0[:, :], 5: pl1[:, :]}

        free = {}

        def bw(b):
            return free.get(("pb", b))

        def tile(buf, i, w=CH):
            return buf[:, i * w:(i + 1) * w]

        DV.add(lambda v: v.memset(ONES[:, :], 1.0))
        DV.add(lambda v: v.memset(OO8[:, :], 0.125))
        DV.add(lambda v: v.memset(CST[:, 0:1], -SHIFT))
        DV.add(lambda v: v.memset(CST[:, 1:2], EPS))
        dv_const = DV.add(lambda v: v.memset(CST[:, 2:3], 1.0 / D), inc=sDV)

        SY.add(lambda s: s.dma_start(
            X[:, :].rearrange("p (i t) -> p i t", i=DT),
            x_d.rearrange("(i p) t -> p i t", p=128)[:, :, 0:CH]),
            inc=sIN, inc_amt=16)
        SY.add(lambda s: s.dma_start(RPC[:, :], ropc_d[:, 0:CH]), inc=sIN, inc_amt=16)
        SY.add(lambda s: s.dma_start(RPS[:, :], rops_d[:, 0:CH]), inc=sIN, inc_amt=16)
        ds_in0 = SY.add(lambda s: s.dma_start(BRc[:, :], br_d), inc=sIN, inc_amt=16)
        ds_x0 = ds_rope0 = ds_br = ds_in0

        GP.add(lambda g: g.dma_start(
            ATW[:, :].rearrange("p (j m) -> p j m", j=DT),
            at_d.rearrange("(j p) m -> p j m", p=128)), inc=sW0, inc_amt=16)
        GP.add(lambda g: g.dma_start(
            WRT[:, :].rearrange("p (j h) -> p j h", j=DT),
            wrt_d.rearrange("(j p) h -> p j h", p=128)), inc=sW0, inc_amt=16)
        GP.add(lambda g: g.dma_start(REP[:, :], rep_d), inc=sW0, inc_amt=16)
        dg_w0v = GP.add(lambda g: g.dma_start(RPM[:, :], rpm_d), inc=sW0, inc_amt=16)
        dg_w0 = None

        x_state = [[(sIN, ds_in0)] * DT, [None] * DT]
        rope_ready = [(sIN, ds_in0), None]
        dv_ctx = {}

        def emit_LN(c):
            pe_s1 = pe_s2 = None
            cast_v = {}
            s2_v = {}
            for i in range(DT):
                cast_v[i] = DV.add(
                    (lambda i=i: lambda v: v.tensor_copy(tile(XN, i), tile(X, i)))(),
                    waits=[x_state[c][i], free.get(("xn", i))], inc=sDV)
                free[("xrd", i)] = (sDV, cast_v[i])
                po_sq = GP.add(
                    (lambda i=i: lambda g: g.tensor_tensor(
                        SQr[:, :], tile(XN, i), tile(XN, i), ALU.mult))(),
                    waits=[(sDV, cast_v[i]), free.get(("sq",))], inc=sPO)
                w1 = [(sDV, cast_v[i])]
                if i == 0:
                    w1 += [bw(4), (sDV, dv_const)]
                pe_s1 = PE.add(
                    (lambda i=i: lambda t: t.matmul(
                        BNK[4], ONES[:, :], tile(XN, i),
                        start=(i == 0), stop=(i == DT - 1)))(),
                    waits=w1, inc=sPE)
                w2 = [(sPO, po_sq)]
                if i == 0:
                    w2 += [bw(5)]
                pe_s2 = PE.add(
                    (lambda i=i: lambda t: t.matmul(
                        BNK[5], ONES[:, :], SQr[:, :],
                        start=(i == 0), stop=(i == DT - 1)))(),
                    waits=w2, inc=sPE)
                s2_v[i] = pe_s2
                free[("sq",)] = (sPE, pe_s2)
            dv_m = DV.add(lambda v: v.tensor_scalar(
                MB[:, :], BNK[4], CST[:, 2:3], None, ALU.mult),
                waits=[(sPE, pe_s1), free.get(("mb",))], inc=sDV)
            dv_mb = DV.add(lambda v: v.tensor_copy(MBB[:, :], MB[:, :]),
                           waits=[(sDV, dv_m), free.get(("mbb",)),
                                  free.get(("mbb2",))], inc=sDV)
            dv_t2 = DV.add(lambda v: v.tensor_tensor(
                G0[:, :], MB[:, :], MB[:, :], ALU.mult),
                waits=[(sDV, dv_m), free.get(("g0",))], inc=sDV)
            dv_v = DV.add(lambda v: v.scalar_tensor_tensor(
                RSD[:, :], BNK[5], CST[:, 2:3], G0[:, :], ALU.mult, ALU.subtract),
                waits=[(sPE, pe_s2), (sDV, dv_t2), free.get(("rsd",))], inc=sDV)
            free[("pb", 4)] = (sDV, dv_m)
            free[("pb", 5)] = (sDV, dv_v)
            free[("mb",)] = (sDV, dv_t2)
            ac_ln = AC.add(lambda sc: sc.activation(
                G0[:, :], RSD[:, :], AF.Ln, bias=CST[:, 1:2]),
                waits=[(sDV, dv_v)], inc=sAC)
            free[("rsd",)] = (sAC, ac_ln)
            ac_r = AC.add(lambda sc: sc.activation(
                RSDB[:, :], G0[:, :], AF.Exp, scale=-0.5),
                waits=[(sAC, ac_ln), free.get(("rsdb",)),
                       free.get(("rsdb2",))], inc=sAC)
            free[("g0",)] = (sAC, ac_r)
            dv_xn = {}
            last_pool = last_dv = None
            for i in range(DT):
                onpool = (i % 2 == 0)
                E = GP if onpool else DV
                sem = sPO if onpool else sDV
                sl = 0 if onpool else 1
                wsub = [(sPE, s2_v[i]), (sDV, dv_mb), free.get(("tts", sl))]
                v_sub = E.add(
                    (lambda i=i, sl=sl: lambda e: e.tensor_tensor(
                        tile(TTr, sl), tile(XN, i), MBB[:, :], ALU.subtract))(),
                    waits=wsub, inc=sem)
                v_mul = E.add(
                    (lambda i=i, sl=sl: lambda e: e.tensor_tensor(
                        tile(XN, i), tile(TTr, sl), RSDB[:, :], ALU.mult))(),
                    waits=[(sem, v_sub), (sAC, ac_r)], inc=sem)
                free[("tts", sl)] = (sem, v_mul)
                dv_xn[i] = (sem, v_mul)
                if onpool:
                    last_pool = v_mul
                else:
                    last_dv = v_mul
            free[("mbb",)] = (sPO, last_pool)
            free[("mbb2",)] = (sDV, last_dv)
            free[("rsdb",)] = (sPO, last_pool)
            free[("rsdb2",)] = (sDV, last_dv)
            return dv_xn

        def emit_FFN(c, dv_xn):
            pe_lg = None
            for j in range(DT):
                w = [dv_xn[j]]
                if j == 0:
                    w += [(sW0, dg_w0v), bw(2)]
                pe_lg = PE.add(
                    (lambda j=j: lambda t: t.matmul(
                        BNK[2][:8, :], WRT[:, j * 8:(j + 1) * 8], tile(XN, j),
                        start=(j == 0), stop=(j == DT - 1)))(),
                    waits=w, inc=sPE)
                free[("xn", j)] = (sPE, pe_lg)
            ac_e = AC.add(lambda sc: sc.activation(
                E8[:, :], BNK[2][:8, :], AF.Exp, bias=BRc[:8, :]),
                waits=[(sPE, pe_lg), (sIN, ds_in0), free.get(("e8",))], inc=sAC)
            free[("pb", 2)] = (sAC, ac_e)
            pe_s8 = PE.add(lambda t: t.matmul(
                BNK[4], OO8[:, :], E8[:, :], start=True, stop=True),
                waits=[(sAC, ac_e), bw(4)], inc=sPE)
            pe_eb = PE.add(lambda t: t.matmul(
                BNK[5], REP[:8, :], E8[:, :], start=True, stop=True),
                waits=[bw(5)], inc=sPE)
            free[("e8",)] = (sPE, pe_eb)
            pe_low = None
            for j in range(DT):
                w = [dv_xn[j]] + ([bw(3)] if j == 0 else [])
                pe_low = PE.add(
                    (lambda j=j: lambda t: t.matmul(
                        BNK[3], ATW[:, j * 128:(j + 1) * 128], tile(XN, j),
                        start=(j == 0), stop=(j == DT - 1)))(),
                    waits=w, inc=sPE)
                free[("xn", j)] = (sPE, pe_low)
            dv_rb = DV.add(lambda v: v.reciprocal(RSD[:, :], BNK[4]),
                           waits=[(sPE, pe_s8), free.get(("rsd",))], inc=sDV)
            free[("pb", 4)] = (sDV, dv_rb)
            dv_t1 = DV.add(lambda v: v.tensor_tensor(
                G0[:, :], BNK[5], RSD[:, :], ALU.mult),
                waits=[(sPE, pe_eb), (sDV, dv_rb), free.get(("g0",))], inc=sDV)
            free[("pb", 5)] = (sDV, dv_t1)
            free[("rsd",)] = (sDV, dv_t1)
            lowp = PS4[:, 0:CH]
            dv_lp = DV.add(lambda v: v.tensor_tensor(
                lowp, BNK[3], G0[:, :], ALU.mult),
                waits=[(sPE, pe_low), (sDV, dv_t1), free.get(("ps4", 0))], inc=sDV)
            free[("pb", 3)] = (sDV, dv_lp)
            free[("g0",)] = (sDV, dv_lp)
            bthalf = []
            for hh in range(2):
                wsem = sWB0 if hh == 0 else sWB1
                dg_bt = GP.add(
                    (lambda hh=hh: lambda g: g.dma_start(
                        WB[:, hh * 2048:(hh + 1) * 2048],
                        bt_d[:, hh * 2048:(hh + 1) * 2048]))(),
                    waits=[free.get(("wb", hh))], inc=wsem, inc_amt=16)
                bthalf.append((wsem, dg_bt))
            obank = [0, 1, 6, 7]
            pe_half = [None, None]
            pe_last = None
            for i in range(DT):
                b = obank[i % 4]
                hh = 0 if i < 16 else 1
                w = [(sDV, dv_lp), bw(b)]
                if i in (0, 16):
                    w.append(bthalf[hh])
                pe_o = PE.add(
                    (lambda i=i, b=b, hh=hh: lambda t: t.matmul(
                        BNK[b],
                        WB[:, hh * 2048 + (i % 16) * 128:
                              hh * 2048 + (i % 16) * 128 + 128],
                        lowp, start=True, stop=True))(),
                    waits=w, inc=sPE)
                pe_half[hh] = pe_o
                pe_last = pe_o
                dv_add = DV.add(
                    (lambda i=i, b=b: lambda v: v.tensor_tensor(
                        tile(X, i), BNK[b], tile(X, i), ALU.add))(),
                    waits=[(sPE, pe_o), free.get(("xrd", i))], inc=sDV)
                free[("pb", b)] = (sDV, dv_add)
                x_state[c][i] = (sDV, dv_add)
            free[("ps4", 0)] = (sPE, pe_last)
            free[("wb", 0)] = (sPE, pe_half[0])
            free[("wb", 1)] = (sPE, pe_half[1])

        def emit_QT(c, dv_xn, cb_pre):
            po_qt = {}
            dv_qe = {}
            pe_qv = {}

            def rope_tail(i):
                rb_ = 6 + (i % 2)
                pe_rot = PE.add(
                    (lambda i=i, rb_=rb_: lambda t: t.matmul(
                        BNK[rb_], RPM[:, :], tile(QC, i),
                        start=True, stop=True))(),
                    waits=[(sDV, dv_qe[i]), bw(rb_), (sW0, dg_w0v)], inc=sPE)
                dv_t = DV.add(
                    (lambda i=i: lambda v: v.tensor_tensor(
                        G0[:, :], tile(QC, i), RPC[:, :], ALU.mult))(),
                    waits=[(sDV, dv_qe[i]), rope_ready[c], free.get(("g0",))],
                    inc=sDV)
                dv_u = DV.add(
                    (lambda rb_=rb_: lambda v: v.tensor_tensor(
                        UR[:, :], BNK[rb_], RPS[:, :], ALU.mult))(),
                    waits=[(sPE, pe_rot), free.get(("ur",))], inc=sDV)
                free[("pb", rb_)] = (sDV, dv_u)
                po_qt[i] = DV.add(
                    (lambda i=i: lambda v: v.tensor_tensor(
                        tile(QC, i), G0[:, :], UR[:, :], ALU.add))(),
                    waits=[(sDV, dv_t), (sDV, dv_u)], inc=sDV)
                free[("ur",)] = (sDV, po_qt[i])
                free[("g0",)] = (sDV, po_qt[i])

            for i in range(DT):
                if i == 0 and cb_pre is not None:
                    dghalf = cb_pre
                elif c > 0:
                    dghalf = []
                    for hh in range(2):
                        wsem = sWS0 if hh == 0 else sWS1
                        dg_cb = AC.add(
                            (lambda i=i, hh=hh: lambda sc: sc.dma_start(
                                WB[:, hh * 2048:(hh + 1) * 2048],
                                wqb_d[2 * i + hh]))(),
                            waits=[free.get(("wb", hh)), pco_done.get("q")],
                            inc=wsem, inc_amt=16)
                        dghalf.append((wsem, dg_cb))
                else:
                    dghalf = []
                    for hh in range(2):
                        wsem = sWB0 if hh == 0 else sWB1
                        dg_cb = GP.add(
                            (lambda i=i, hh=hh: lambda g: g.dma_start(
                                WB[:, hh * 2048:(hh + 1) * 2048]
                                .rearrange("p (j m) -> p j m", j=16),
                                wq_d.rearrange("(j p) m -> p j m", p=128)
                                [:, hh * 16:(hh + 1) * 16, i * 128:(i + 1) * 128]))(),
                            waits=[free.get(("wb", hh))], inc=wsem, inc_amt=16)
                        dghalf.append((wsem, dg_cb))
                b = i % 2
                pe_q = None
                for j in range(DT):
                    w = [dv_xn[j]]
                    if j == 0:
                        w += [dghalf[0], bw(b)]
                    elif j == 16:
                        w += [dghalf[1]]
                    pe_q = PE.add(
                        (lambda i=i, j=j, b=b: lambda t: t.matmul(
                            BNK[b],
                            WB[:, (0 if j < 16 else 2048) + (j % 16) * 128:
                                  (0 if j < 16 else 2048) + (j % 16) * 128 + 128],
                            tile(XN, j), start=(j == 0), stop=(j == DT - 1)))(),
                        waits=w, inc=(sPE if j in (15, DT - 1) else None))
                    if j == 15:
                        free[("wb", 0)] = (sPE, pe_q)
                free[("wb", 1)] = (sPE, pe_q)
                pe_qv[i] = pe_q
                dv_qe[i] = DV.add(
                    (lambda i=i, b=b: lambda v: v.tensor_copy(
                        tile(QC, i), BNK[b]))(),
                    waits=[(sPE, pe_q), free.get(("qc_all",))], inc=sDV)
                free[("pb", b)] = (sDV, dv_qe[i])
                if i >= 1:
                    rope_tail(i - 1)
            rope_tail(DT - 1)
            free[("rope",)] = (sDV, po_qt[DT - 1])
            for j in range(DT):
                free[("xn", j)] = (sPE, pe_qv[DT - 1])
            return po_qt

        pc_queue = []
        pco_done = {}

        def precast_pump(n=1):
            for _ in range(n):
                if not pc_queue:
                    return
                which, blk = pc_queue.pop(0)
                i, hh = blk // 2, blk % 2
                w_d = wo_d if which == "o" else wq_d
                dst_d = wob_d if which == "o" else wqb_d
                osem = sPCOo if which == "o" else sPCOq
                GP.add(
                    (lambda i=i, hh=hh, w_d=w_d: lambda g: g.dma_start(
                        PCB[:, :].rearrange("p (j m) -> p j m", j=16),
                        w_d.rearrange("(j p) m -> p j m", p=128)
                        [:, hh * 16:(hh + 1) * 16, i * 128:(i + 1) * 128]))(),
                    waits=[free.get(("pcb",))], inc=sPCI, inc_amt=16)
                v = SY.add(
                    (lambda blk=blk, dst_d=dst_d: lambda s: s.dma_start(
                        dst_d[blk], PCB[:, :]))(),
                    waits=[(sPCI, sPCI.val)], inc=osem, inc_amt=16)
                free[("pcb",)] = (osem, v)
                pco_done[which] = (osem, v)

        def emit_kv_slab(c, g, s, ktv, vtv):
            v = GP.add(
                (lambda g=g, s=s: lambda gp: gp.dma_start(
                    KT[:, s * SLAB * 128:(s + 1) * SLAB * 128],
                    kt_d[g * 128:(g + 1) * 128,
                         s * SLAB * 128:(s + 1) * SLAB * 128]))(),
                waits=[free.get(("kvslab", s))], inc=sKT[s], inc_amt=16)
            ktv[(g, s)] = (sKT[s], v)
            v = GP.add(
                (lambda g=g, s=s: lambda gp: gp.dma_start(
                    VT[:, s * SLAB * 128:(s + 1) * SLAB * 128]
                    .rearrange("p (j d) -> p j d", j=SLAB),
                    vt_d.rearrange("(j p) d -> p j d", p=128)
                    [:, g * NK + s * SLAB:g * NK + (s + 1) * SLAB, :]))(),
                waits=[free.get(("kvslab", s))], inc=sVT[s], inc_amt=16)
            vtv[(g, s)] = (sVT[s], v)

        def emit_kv_dma(c, g, ktv, vtv):
            for s in range(NSLAB):
                emit_kv_slab(c, g, s, ktv, vtv)

        flushctr = [0]

        def emit_ATT(c, po_qt):
            ktv, vtv = {}, {}
            emit_kv_dma(c, 0, ktv, vtv)
            ggl = [0]
            pending = []

            def emit_scores_group(g, qh, grp):
                h = g * 4 + qh
                gg = ggl[0]
                ggl[0] += 1
                lo = 0 if (gg % 2 == 0) else 6
                ps_val = None
                for t in range(2):
                    j = 2 * grp + t
                    w = [ktv[(g, j // SLAB)]]
                    if t == 0:
                        w += [bw(lo), bw(lo + 1), (sDV, po_qt[h])]
                    ps_val = PE.add(
                        (lambda j=j, b=lo + t, h=h: lambda tt: tt.matmul(
                            BNK[b], KT[:, j * 128:(j + 1) * 128],
                            tile(QC, h), start=True, stop=True))(),
                        waits=w, inc=(sPS if t == 1 else None))
                slot = (2 * gg) % NPS
                big = psA if (gg % 2 == 0) else psB
                ac_e = AC.add(
                    (lambda slot=slot, big=big: lambda sc: sc.activation(
                        PS4[:, slot * CH:(slot + 2) * CH], big[:, :], AF.Exp,
                        bias=CST[:, 0:1], scale=ISQ))(),
                    waits=[(sPS, ps_val)], inc=sAC)
                free[("pb", lo)] = (sAC, ac_e)
                free[("pb", lo + 1)] = (sAC, ac_e)
                return gg, ac_e

            def flush_group(item):
                g, qh, grp, gg, ac_e = item
                h = g * 4 + qh
                cb = 2 + (h % 2)
                lb = 4 + (h % 2)
                slot = (2 * gg) % NPS
                pe_L = pe_ctx = None
                for t in range(2):
                    j = 2 * grp + t
                    wl = [(sAC, ac_e)]
                    if j == 0:
                        wl.append(bw(lb))
                    pe_L = PE.add(
                        (lambda lb=lb, slot=slot, t=t, j=j: lambda tt: tt.matmul(
                            BNK[lb], ONES[:, :],
                            PS4[:, (slot + t) * CH:(slot + t + 1) * CH],
                            start=(j == 0), stop=(j == NK - 1)))(),
                        waits=wl, inc=(sPE if j == NK - 1 else None))
                    wc = [vtv[(g, j // SLAB)]]
                    if j == 0:
                        wc.append(bw(cb))
                    need_inc = (t == 1) and (
                        (qh == 3 and (j % SLAB) == SLAB - 1)
                        or grp == NK // 2 - 1)
                    pe_ctx = PE.add(
                        (lambda g=g, j=j, slot=slot, t=t, cb=cb: lambda tt: tt.matmul(
                            BNK[cb], VT[:, j * 128:(j + 1) * 128],
                            PS4[:, (slot + t) * CH:(slot + t + 1) * CH],
                            start=(j == 0), stop=(j == NK - 1)))(),
                        waits=wc, inc=(sPE if need_inc else None))
                    if qh == 3 and t == 1 and (j % SLAB) == SLAB - 1:
                        free[("kvslab", j // SLAB)] = (sPE, pe_ctx)
                        if g + 1 < NG:
                            emit_kv_slab(c, g + 1, j // SLAB, ktv, vtv)
                if flushctr[0] % 11 == 0:
                    precast_pump(1)
                flushctr[0] += 1
                if grp == NK // 2 - 1:
                    ac_lnl = AC.add(
                        (lambda lb=lb: lambda sc: sc.activation(
                            MB[:, :], BNK[lb], AF.Ln))(),
                        waits=[(sPE, pe_L), free.get(("mb",))], inc=sAC)
                    free[("pb", lb)] = (sAC, ac_lnl)
                    ac_rl = AC.add(lambda sc: sc.activation(
                        RSD[:, :], MB[:, :], AF.Exp, scale=-1.0),
                        waits=[(sAC, ac_lnl), free.get(("rsd",))], inc=sAC)
                    free[("mb",)] = (sAC, ac_rl)
                    dv_c = DV.add(
                        (lambda h=h, cb=cb: lambda v: v.tensor_tensor(
                            tile(QC, h), BNK[cb], RSD[:, :], ALU.mult))(),
                        waits=[(sPE, pe_ctx), (sAC, ac_rl)], inc=sDV)
                    free[("pb", cb)] = (sDV, dv_c)
                    free[("rsd",)] = (sDV, dv_c)
                    dv_ctx[h] = dv_c

            for g in range(NG):
                for qh in range(4):
                    for grp in range(NK // 2):
                        gg, ac_e = emit_scores_group(g, qh, grp)
                        pending.append((g, qh, grp, gg, ac_e))
                        if len(pending) > 2:
                            flush_group(pending.pop(0))
                while pending:
                    flush_group(pending.pop(0))
            free[("ps4", 0)] = (sPE, sPE.val)

        def emit_WO(c):
            ctx_all = (sDV, dv_ctx[NH - 1])
            pe_last = None
            for i in range(DT):
                dghalf = []
                for hh in range(2):
                    wsem = sWS0 if hh == 0 else sWS1
                    dg_cb = AC.add(
                        (lambda i=i, hh=hh: lambda sc: sc.dma_start(
                            WB[:, hh * 2048:(hh + 1) * 2048],
                            wob_d[2 * i + hh]))(),
                        waits=[free.get(("wb", hh)), pco_done.get("o")],
                        inc=wsem, inc_amt=16)
                    dghalf.append((wsem, dg_cb))
                b = i % 2
                pe_o = None
                for j in range(DT):
                    w = []
                    if j == 0:
                        w = [dghalf[0], ctx_all, bw(b)]
                    elif j == 16:
                        w = [dghalf[1]]
                    pe_o = PE.add(
                        (lambda i=i, j=j, b=b: lambda t: t.matmul(
                            BNK[b],
                            WB[:, (0 if j < 16 else 2048) + (j % 16) * 128:
                                  (0 if j < 16 else 2048) + (j % 16) * 128 + 128],
                            tile(QC, j), start=(j == 0), stop=(j == DT - 1)))(),
                        waits=w, inc=(sPE if j in (15, DT - 1) else None))
                    if j == 15:
                        free[("wb", 0)] = (sPE, pe_o)
                free[("wb", 1)] = (sPE, pe_o)
                pe_last = pe_o
                dv_add = DV.add(
                    (lambda i=i, b=b: lambda v: v.tensor_tensor(
                        tile(X, i), BNK[b], tile(X, i), ALU.add))(),
                    waits=[(sPE, pe_o), free.get(("xrd", i))], inc=sDV)
                free[("pb", b)] = (sDV, dv_add)
                x_state[c][i] = (sDV, dv_add)
            free[("qc_all",)] = (sPE, pe_last)

        def emit_STORE(c):
            sta = stb = None
            for i in range(DT):
                ch = sST if (c == 1 or i < DT // 2) else sSTb
                v = SY.add(
                    (lambda i=i, c=c: lambda s: s.dma_start(
                        y_d.rearrange("(i p) t -> p i t", p=128)
                        [:, i, c * CH:(c + 1) * CH],
                        tile(X, i)))(),
                    waits=[x_state[c][i]], inc=ch, inc_amt=16)
                if ch is sST:
                    sta = v
                else:
                    stb = v
            if c == 0:
                lda = ldb = None
                for i in range(DT):
                    ch = sXL if i < DT // 2 else sXLb
                    v = SY.add(
                        (lambda i=i: lambda s: s.dma_start(
                            tile(X, i),
                            x_d.rearrange("(i p) t -> p i t", p=128)
                            [:, i, CH:2 * CH]))(),
                        waits=[(sST, sta) if i < DT // 2 else (sSTb, stb)],
                        inc=ch, inc_amt=16)
                    if ch is sXL:
                        lda = v
                    else:
                        ldb = v
                for i in range(DT):
                    x_state[1][i] = (sXL, lda) if i < DT // 2 else (sXLb, ldb)

        for c in range(NCH):
            if c > 0:
                SY.add(lambda s: s.dma_start(RPC[:, :], ropc_d[:, CH:2 * CH]),
                       waits=[free.get(("rope",))], inc=sRP, inc_amt=16)
                rv = SY.add(lambda s: s.dma_start(RPS[:, :], rops_d[:, CH:2 * CH]),
                            inc=sRP, inc_amt=16)
                rope_ready[c] = (sRP, rv)
            dv_xn = emit_LN(c)
            emit_FFN(c, dv_xn)
            cb_pre = None
            if c == 0:
                cb_pre = []
                for hh in range(2):
                    wsem = sWB0 if hh == 0 else sWB1
                    cb_pre.append((wsem, GP.add(
                        (lambda hh=hh: lambda g: g.dma_start(
                            WB[:, hh * 2048:(hh + 1) * 2048]
                            .rearrange("p (j m) -> p j m", j=16),
                            wq_d.rearrange("(j p) m -> p j m", p=128)
                            [:, hh * 16:(hh + 1) * 16, 0:128]))(),
                        waits=[free.get(("wb", hh))], inc=wsem, inc_amt=16)))
            dv_xn = emit_LN(c)
            po_qt = emit_QT(c, dv_xn, cb_pre)
            if c == 0:
                pc_queue.extend([("o", b) for b in range(2 * DT)])
                pc_queue.extend([("q", b) for b in range(2 * DT)])
            emit_ATT(c, po_qt)
            while pc_queue:
                precast_pump(1)
            emit_WO(c)
            dv_xn = emit_LN(c)
            emit_FFN(c, dv_xn)
            emit_STORE(c)

        SY.add(None, waits=[(sST, sST.val), (sSTb, sSTb.val)])

        @block.tensor
        def _(t):
            PE.replay(t)

        @block.scalar
        def _(sc):
            AC.replay(sc)

        @block.vector
        def _(v):
            DV.replay(v)

        @block.gpsimd
        def _(g):
            GP.replay(g)

        @block.sync
        def _(s):
            SY.replay(s)

    return nc


_CACHE = {}


def _get_program():
    if "nc" not in _CACHE:
        _CACHE["nc"] = build_program()
    return _CACHE["nc"]


def kernel(L0_attn_hidden_states, L0_attn_Wq, L0_attn_Wo, L0_attn_kv_cache,
           L0_attn_block_table, L0_ffn_A, L0_ffn_B, L0_ffn_Wr, L0_ffn_br):
    hs = np.ascontiguousarray(np.asarray(L0_attn_hidden_states, dtype=np.float32))
    wq = np.ascontiguousarray(np.asarray(L0_attn_Wq, dtype=np.float32))
    wo = np.ascontiguousarray(np.asarray(L0_attn_Wo, dtype=np.float32))
    kv = np.asarray(L0_attn_kv_cache, dtype=np.float32)
    btab = np.asarray(L0_attn_block_table, dtype=np.int64)
    A = np.asarray(L0_ffn_A, dtype=np.float32)
    B = np.asarray(L0_ffn_B, dtype=np.float32)
    Wr = np.asarray(L0_ffn_Wr, dtype=np.float32)
    br = np.asarray(L0_ffn_br, dtype=np.float32)

    blocks = kv[btab]
    ktm = np.ascontiguousarray(
        blocks[:, 0].transpose(1, 3, 0, 2).reshape(NG * 128, NKEY))
    vtm = np.ascontiguousarray(
        blocks[:, 1].transpose(1, 0, 2, 3).reshape(NG * NKEY, 128))
    atm = np.ascontiguousarray(A.reshape(128, D).T)
    btm = np.ascontiguousarray(B.transpose(0, 2, 1).reshape(128, D))
    wrtm = np.ascontiguousarray(Wr.T)
    brm = np.zeros((128, 1), np.float32); brm[:8, 0] = br

    rep = np.zeros((128, 128), np.float32)
    for h in range(8):
        rep[h, h * 16:(h + 1) * 16] = 1.0
    rpm = np.zeros((128, 128), np.float32)
    for m in range(64):
        rpm[m + 64, m] = -1.0
    for m in range(64, 128):
        rpm[m - 64, m] = 1.0
    invf = (10000.0 ** (-np.arange(64, dtype=np.float64) / 64.0))

    hs2 = hs.reshape(2 * 4096, D)
    in_maps = []
    for cid in range(NCORE):
        toks = cid * TOK + np.arange(TOK)
        pos = (toks % 4096).astype(np.float64)
        ang = pos[None, :] * invf[:, None]
        ang = np.concatenate([ang, ang], axis=0)
        ropc = np.cos(ang).astype(np.float32)
        rops = np.sin(ang).astype(np.float32)
        xc = np.ascontiguousarray(hs2[cid * TOK:(cid + 1) * TOK].T)
        in_maps.append({
            "x": xc, "kt": ktm, "vt": vtm, "wq": wq, "wo": wo,
            "at": atm, "bt": btm, "wrt": wrtm, "br": brm,
            "rep": rep, "rpm": rpm, "ropc": np.ascontiguousarray(ropc),
            "rops": np.ascontiguousarray(rops),
        })

    nc = _get_program()
    res = run_bass_kernel_spmd(nc, in_maps, core_ids=list(range(NCORE)))
    out = np.empty((2 * 4096, D), np.float32)
    for cid in range(NCORE):
        out[cid * TOK:(cid + 1) * TOK] = np.asarray(
            res.results[cid]["y"], dtype=np.float32).T
    return out.reshape(2, 4096, D)
